# version 19
# baseline (speedup 1.0000x reference)
import contextlib
import numpy as np
import ml_dtypes
import concourse.bass as bass
import concourse.mybir as mybir
from concourse.bass_utils import run_bass_kernel_spmd
from concourse.alu_op_type import AluOpType as OP

F32 = mybir.dt.float32
BF16 = mybir.dt.bfloat16
U32 = mybir.dt.uint32
U16 = mybir.dt.uint16
I32 = mybir.dt.int32
AF = mybir.ActivationFunctionType
AX = mybir.AxisListType.X
BF = ml_dtypes.bfloat16

NCORES = 8
T = 1024
D = 4096
INTER = 11008
RPD = 1376
NPC = 88
NSUP = 8

PCS = []
_c0 = 0
for _qp in range(11):
    _rp0 = 128 * _qp
    _rpw = min(128, RPD - _rp0)
    for _jp in range(8):
        PCS.append((_qp, _jp, _rp0, _rpw, _c0))
        _c0 += 2 * _rpw
assert _c0 == 2 * INTER

JLIST = [0, 4, 1, 5, 2, 6, 3, 7]
JPOS = {j: k for k, j in enumerate(JLIST)}

PI = np.zeros(D, dtype=np.int64)
for _q in range(4):
    for _j in range(8):
        _c = _q * 8 + _j
        PI[_c * 128:(_c + 1) * 128] = 8 * (128 * _q + np.arange(128)) + _j

COLPERM = np.zeros(2 * INTER, dtype=np.int64)
for (_qp, _jp, _rp0, _rpw, _cc) in PCS:
    rps = _rp0 + np.arange(_rpw)
    COLPERM[_cc:_cc + _rpw] = 8 * rps + _jp
    COLPERM[_cc + _rpw:_cc + 2 * _rpw] = INTER + 8 * rps + _jp

ANDMASKS = [(0x000F000F << (4 * s)) & 0xFFFFFFFF for s in range(4)]


class Sched:
    ENGS = ("sync", "vector", "scalar", "tensor", "gpsimd")

    sems = ("sq", "gq", "aq", "ve", "ac", "pe", "gp", "sqa", "gqa", "aqa")

    ACKQ = {"sq": ("sync", "sqa"), "gq": ("gpsimd", "gqa"), "aq": ("scalar", "aqa")}

    def __init__(self):
        self.ops = {e: [] for e in self.ENGS}
        self.count = {s: 0 for s in self.sems}
        self.events = {}
        self.acks = {"sq": [], "gq": [], "aq": []}
        self.semh = None

    def _ack_for(self, sm, val):
        for c, nm in self.acks[sm]:
            if c >= val:
                return nm
        eng, asem = self.ACKQ[sm]
        cur = self.count[sm]
        nm = f"__ack_{sm}_{cur}"

        def fn(e, _sm=sm, _cur=cur):
            return e.wait_ge(self.semh[_sm], _cur)
        self.count[asem] += 1
        self.events[nm] = (asem, self.count[asem])
        self.ops[eng].append((fn, (), nm, asem, 1))
        self.acks[sm].append((cur, nm))
        return nm

    def add(self, eng, fn, waits=(), ev=None, sem=None, amt=1):
        rw = []
        for w in waits:
            if isinstance(w, str) and w.startswith("NOW:"):
                sm = w[4:]
                nm = f"__now_{sm}_{len(self.ops[eng])}_{eng}_{self.count[sm]}"
                self.events[nm] = (sm, self.count[sm])
                rw.append(nm)
            else:
                assert w in self.events, f"wait on undefined event {w} (eng {eng})"
                sm, val = self.events[w]
                if sm in ("sq", "gq", "aq"):
                    rw.append(self._ack_for(sm, val))
                else:
                    rw.append(w)
        if ev is not None:
            assert sem is not None
            self.count[sem] += amt
            self.events[ev] = (sem, self.count[sem])
        self.ops[eng].append((fn, tuple(rw), ev, sem, amt))

    def verify_no_deadlock(self):
        pos = {e: 0 for e in self.ENGS}
        cnt = {s: 0 for s in self.sems}
        progress = True
        while progress:
            progress = False
            for e in self.ENGS:
                while pos[e] < len(self.ops[e]):
                    fn, waits, ev, sem, amt = self.ops[e][pos[e]]
                    if all(cnt[self.events[w][0]] >= self.events[w][1] for w in waits):
                        if ev is not None:
                            cnt[sem] += amt
                        pos[e] += 1
                        progress = True
                    else:
                        break
        stuck = {e: (pos[e], len(self.ops[e])) for e in self.ENGS if pos[e] < len(self.ops[e])}
        if stuck:
            detail = {}
            for e in stuck:
                fn, waits, ev, sem, amt = self.ops[e][pos[e]]
                detail[e] = [(w, self.events[w]) for w in waits
                             if cnt[self.events[w][0]] < self.events[w][1]]
            raise RuntimeError(f"schedule deadlock: stuck={stuck} detail={detail}")


def build_program(debug=False, p2_serial=0):
    nc = bass.Bass("TRN2", target_bir_lowering=False)

    xp = nc.dram_tensor("xp", [T, D], F32, kind="ExternalInput").ap()
    xr = nc.dram_tensor("xr", [T, D], F32, kind="ExternalInput").ap()
    wp = nc.dram_tensor("wp", [D, D], F32, kind="ExternalInput").ap()
    msk = nc.dram_tensor("msk", [D, D], I32, kind="ExternalInput").ap()
    qwp = nc.dram_tensor("qwp", [512, 2 * INTER], U32, kind="ExternalInput").ap()
    scp = nc.dram_tensor("scp", [32, 2 * INTER], F32, kind="ExternalInput").ap()
    scpe = nc.dram_tensor("scpe", [4, 128, 2 * INTER], F32, kind="ExternalInput").ap()
    qzg = nc.dram_tensor("qzg", [32, 2752], U32, kind="ExternalInput").ap()
    qwd = nc.dram_tensor("qwd", [RPD, D], U32, kind="ExternalInput").ap()
    qzd = nc.dram_tensor("qzd", [86, 512], U32, kind="ExternalInput").ap()
    scd = nc.dram_tensor("scd", [86, D], F32, kind="ExternalInput").ap()
    scde = nc.dram_tensor("scde", [11, 128, D], F32, kind="ExternalInput").ap()
    em = nc.dram_tensor("em", [32, 128, 128], BF16, kind="ExternalInput").ap()
    emq = nc.dram_tensor("emq", [4, 128, 128], BF16, kind="ExternalInput").ap()
    exg = nc.dram_tensor("exg", [4, 128, 32], BF16, kind="ExternalInput").ap()
    ehg = nc.dram_tensor("ehg", [11, 128, 86], BF16, kind="ExternalInput").ap()
    idb = nc.dram_tensor("idb", [128, 128], BF16, kind="ExternalInput").ap()
    idf = nc.dram_tensor("idf", [128, 128], F32, kind="ExternalInput").ap()
    out = nc.dram_tensor("out", [T, D], F32, kind="ExternalOutput").ap()
    hsc = nc.dram_tensor("hsc", [11, 128, 8 * T], BF16, kind="Internal").ap()
    dbg = {}
    if debug:
        dbg["xn"] = nc.dram_tensor("d_xn", [128, 32 * T], BF16, kind="ExternalOutput").ap()
        dbg["xg"] = nc.dram_tensor("d_xg", [32, T], BF16, kind="ExternalOutput").ap()
        dbg["hg"] = nc.dram_tensor("d_hg", [86, T], BF16, kind="ExternalOutput").ap()
        dbg["bm"] = nc.dram_tensor("d_bm", [128, 128], BF16, kind="ExternalOutput").ap()
        dbg["h0"] = nc.dram_tensor("d_h0", [128, T], BF16, kind="ExternalOutput").ap()
        dbg["g0"] = nc.dram_tensor("d_g0", [128, T], F32, kind="ExternalOutput").ap()
        dbg["c0"] = nc.dram_tensor("d_c0", [128, T], F32, kind="ExternalOutput").ap()
        dbg["g1"] = nc.dram_tensor("d_g1", [128, T], F32, kind="ExternalOutput").ap()
        dbg["hin"] = nc.dram_tensor("d_hin", [128, 2 * T], BF16, kind="ExternalOutput").ap()
        dbg["wdd"] = nc.dram_tensor("d_wdd", [128, 1024], BF16, kind="ExternalOutput").ap()
        dbg["szq"] = nc.dram_tensor("d_szq", [86, 512], BF16, kind="ExternalOutput").ap()
        dbg["c1"] = nc.dram_tensor("d_c1", [128, T], F32, kind="ExternalOutput").ap()

    with contextlib.ExitStack() as ctx:
        def sbuf(name, shape, dt):
            return ctx.enter_context(nc.sbuf_tensor(name, shape, dt))

        arena = sbuf("arena", [128, 49152], mybir.dt.uint8)
        _DTB = {F32: 4, BF16: 2, U32: 4, U16: 2, I32: 4}

        def ar(off, dt, cols):
            return arena[:, off:off + cols * _DTB[dt]].bitcast(dt)

        v_xt = [ar(0, F32, 4096), ar(16384, F32, 4096)]
        v_xntok = [ar(32768, BF16, 4096), ar(40960, BF16, 4096)]
        v_wdeq = [ar(0, BF16, 8192), ar(16384, BF16, 8192)]
        v_and = ar(32768, U32, 2048)
        a_silu = ar(40960, F32, 1024)
        a_hact = [ar(45056, BF16, 1024), ar(47104, BF16, 1024)]
        v_wrow = ar(0, F32, 4096)
        v_wb = ar(16384, BF16, 4096)
        v_wtb = ar(24576, BF16, 4096)
        v_comb = ar(32768, F32, 4096)

        xn = sbuf("xn", [128, 32 * T], BF16)
        qw_t = [sbuf(f"qw{b}", [128, 1024], U32) for b in range(2)]
        sc_t = [sbuf(f"sct{b}", [128, 1024], F32) for b in range(2)]
        sc_c = [sbuf(f"scc{b}", [32, 512], F32) for b in range(2)]
        qz_t = [sbuf(f"qzt{b}", [32, 256], U32) for b in range(2)]
        zq_s = sbuf("zqs", [32, 256], U32)
        szqm = [sbuf(f"szq{b}", [32, 256], BF16) for b in range(2)]
        xgtn = sbuf("xgtn", [32, T], BF16)
        hgn = sbuf("hgn", [86, T], BF16)
        mask_t = sbuf("mskt", [128, 2048], I32)
        colsum = [sbuf(f"cs{b}", [128, 128], F32) for b in range(2)]
        colsb = [sbuf(f"csb{b}", [128, 128], BF16) for b in range(4)]
        bm_sb = sbuf("bm", [128, 128], BF16)
        bmt_sb = sbuf("bmt", [128, 128], BF16)
        bmx = sbuf("bmx", [128, 512], BF16)
        hin = [sbuf(f"hin{b}", [128, 2 * T], BF16) for b in range(2)]
        qwd_t = [sbuf(f"qwdt{b}", [128, 512], U32) for b in range(2)]
        and_d = [sbuf(f"andd{b}", [128, 512], U32) for b in range(2)]
        wdeq_d = [sbuf(f"wdd{b}", [128, 1024], BF16) for b in range(2)]
        wdeq_d.append(qw_t[0][:].bitcast(BF16)[:, 0:1024])
        wdeq_d.append(qw_t[1][:].bitcast(BF16)[:, 0:1024])
        wdeq_d.append(sc_t[0][:].bitcast(BF16)[:, 0:1024])
        wdeq_d.append(sc_t[0][:].bitcast(BF16)[:, 1024:2048])
        wdeq_d.append(sc_t[1][:].bitcast(BF16)[:, 0:1024])
        wdeq_d.append(sc_t[1][:].bitcast(BF16)[:, 1024:2048])
        scd_e = [sbuf(f"scde{b}", [128, 512], F32) for b in range(2)]
        szqd = sbuf("szqd", [86, 512], BF16)
        g_sb = sbuf("gsb", [128, 4 * T], F32)
        osb = [sbuf(f"osb{b}", [128, 512], F32) for b in range(2)]
        e_mask = [sbuf(f"emask{b}", [128, 128], BF16) for b in range(2)]
        e_mq = sbuf("emqs", [128, 4 * 128], BF16)
        e_xg = sbuf("exgs", [128, 4 * 32], BF16)
        e_hg = sbuf("ehgs", [128, 11 * 86], BF16)
        id_b = sbuf("idb_s", [128, 128], BF16)
        id_f = sbuf("idf_s", [128, 128], F32)
        ln_sm = [sbuf(f"lnsm{b}", [128, 1], F32) for b in range(2)]
        ln_pp = [[sbuf(f"lnpp{q}_{b}", [128, 1], F32) for b in range(2)] for q in range(4)]
        ln_ss = [sbuf(f"lnss{b}", [128, 1], F32) for b in range(2)]
        ln_v2 = [sbuf(f"lnv2{b}", [128, 1], F32) for b in range(2)]
        ln_var = [sbuf(f"lnv{b}", [128, 1], F32) for b in range(2)]
        ln_std = [sbuf(f"lnst{b}", [128, 1], F32) for b in range(2)]
        ln_rstd = [sbuf(f"lnr{b}", [128, 1], F32) for b in range(2)]
        ln_nb = [sbuf(f"lnnb{b}", [128, 1], F32) for b in range(2)]
        acf = sbuf("acf", [128, 1], F32)
        sqh = sbuf("sqh", [128, 1024], F32)
        zqd_s = mask_t[0:86, 0:512].bitcast(U32)
        qzd_t = mask_t[0:86, 512:576].bitcast(U32)
        scd_n = mask_t[0:86, 576:1088].bitcast(F32)
        x2 = [mask_t[:, 1088:1600].bitcast(F32), sqh[:, 512:1024]]
        mtr = sqh[:, 0:1024].bitcast(I32)

        ps = ctx.enter_context(nc.psum_tensor("ps", [128, 4096], F32))
        p_xtr = [ps[:, 0:512].bitcast(BF16), ps[:, 512:1024].bitcast(BF16)]
        p_xg = ps[0:32, 1024:2048]
        p_gate = ps[:, 0:1024]
        p_up = ps[:, 1024:2048]
        p_hg = ps[0:86, 2048:3072]
        p_bm = ps[:, 3072:3200]
        p_bmt = ps[:, 3584:3648].bitcast(BF16)
        p_bmxf = ps[:, 3712:3840]
        p_o2 = [ps[:, hc * 1024:(hc + 1) * 1024] for hc in range(4)]
        p_wt = [ps[:, 0:512], ps[:, 512:1024], ps[:, 3072:3584], ps[:, 3584:4096]]
        p_sp = [ps[:, 1024:2048], ps[:, 2048:3072]]
        p_ot = [ps[:, 2048:2560], ps[:, 2560:3072]]

        sems = {}
        for s_ in Sched.sems:
            sems[s_] = ctx.enter_context(nc.semaphore(s_))

        S = Sched()

        def dma(eng, sem, outap, inap, waits=(), ev=None):
            def fn(e, _o=outap, _i=inap):
                return e.dma_start(out=_o, in_=_i)
            S.add(eng, fn, waits=waits, ev=ev, sem=sem, amt=16)

        def op(eng, sem, fn, waits=(), ev=None):
            S.add(eng, fn, waits=waits, ev=ev, sem=sem if ev else None)

        op("scalar", "ac", lambda s_: s_.memzero(acf[:]), waits=(), ev="acf0")

        dma("sync", "sq", e_mq[:], emq.rearrange("c p m -> p c m"), ev="c_emq")
        dma("sync", "sq", e_xg[:], exg.rearrange("c p m -> p c m"), ev="c_exg")
        dma("sync", "sq", e_hg[:], ehg.rearrange("c p m -> p c m"), ev="c_ehg")
        dma("sync", "sq", id_b[:], idb, ev="c_idb")
        dma("sync", "sq", id_f[:], idf, ev="c_idf")

        def p0_tr(tc):
            for hb in range(4):
                par = hb % 2
                w = [f"lnL{tc}", "c_idb"]
                if hb >= 2:
                    w.append(f"p0ev_{tc}_{hb - 2}")
                elif tc >= 1:
                    w.append(f"p0ev_{tc - 1}_{hb + 2}")

                def tr(t, tc=tc, hb=hb, par=par):
                    r = None
                    for i in range(8):
                        c = hb * 8 + i
                        r = t.transpose(p_xtr[par][:, i * 128:(i + 1) * 128],
                                        v_xntok[tc % 2][:, c * 128:(c + 1) * 128], id_b[:])
                    return r
                op("tensor", "pe", tr, waits=w,
                   ev=f"p0tr_{tc}" if hb == 3 else f"p0trh_{tc}_{hb}")

                def tev(v, tc=tc, hb=hb, par=par):
                    dst = xn[:, :].rearrange("p (c t) -> p c t", c=32)[
                        :, hb * 8:(hb + 1) * 8, tc * 128:(tc + 1) * 128]
                    return v.tensor_copy(dst, p_xtr[par][:].rearrange("p (c t) -> p c t", c=8))
                op("vector", "ve", tev,
                   waits=[f"p0tr_{tc}" if hb == 3 else f"p0trh_{tc}_{hb}"],
                   ev=f"p0ev_{tc}_{hb}")

        for tc in range(8):
            b = tc % 2
            w_ld = [f"lnL{tc - 2}"] if tc >= 2 else []
            dma("sync", "sq", v_xt[b][:], xp[tc * 128:(tc + 1) * 128, :], waits=w_ld,
                ev=f"xld{tc}")

            def lnA(v, b=b):
                return v.reduce_sum(ln_sm[b][:], v_xt[b][:], axis=AX)
            op("vector", "ve", lnA, waits=[f"xld{tc}"], ev=f"ln0{tc}")

            prev = f"ln0{tc}"
            for qq_ in range(4):
                def lnSq(v, b=b, qq_=qq_):
                    return v.tensor_tensor(sqh[:], v_xt[b][:, qq_ * 1024:(qq_ + 1) * 1024],
                                           v_xt[b][:, qq_ * 1024:(qq_ + 1) * 1024], OP.mult)
                op("vector", "ve", lnSq, waits=[prev], ev=f"lnq{tc}_{qq_}")

                def lnRd(v, b=b, qq_=qq_):
                    return v.reduce_sum(ln_pp[qq_][b][:], sqh[:], axis=AX)
                op("vector", "ve", lnRd, waits=[f"lnq{tc}_{qq_}"], ev=f"lnr{tc}_{qq_}")
                prev = f"lnr{tc}_{qq_}"

            def lnE(v, b=b):
                v.tensor_tensor(ln_ss[b][:], ln_pp[0][b][:], ln_pp[1][b][:], OP.add)
                v.tensor_tensor(ln_v2[b][:], ln_pp[2][b][:], ln_pp[3][b][:], OP.add)
                return v.tensor_scalar(ln_sm[b][:], ln_sm[b][:], 1.0 / D, None, OP.mult)
            op("vector", "ve", lnE, waits=[prev], ev=f"lnE{tc}")

            def lnF(v, b=b):
                v.tensor_tensor(ln_var[b][:], ln_ss[b][:], ln_v2[b][:], OP.add)
                return v.tensor_tensor(ln_std[b][:], ln_sm[b][:], ln_sm[b][:], OP.mult)
            op("vector", "ve", lnF, waits=[f"lnE{tc}"], ev=f"lnF{tc}")

            def lnG(v, b=b):
                return v.tensor_scalar(ln_var[b][:], ln_var[b][:], 1.0 / D, None, OP.mult)
            op("vector", "ve", lnG, waits=[f"lnF{tc}"], ev=f"lnG{tc}")

            def lnH0(v, b=b):
                return v.tensor_tensor(ln_var[b][:], ln_var[b][:], ln_std[b][:], OP.subtract)
            op("vector", "ve", lnH0, waits=[f"lnG{tc}"], ev=f"lnH0{tc}")

            def lnH(v, b=b):
                return v.tensor_scalar(ln_var[b][:], ln_var[b][:], 1e-5, None, OP.add)
            op("vector", "ve", lnH, waits=[f"lnH0{tc}"], ev=f"lnH{tc}")

            def sqrt_f(s_, b=b):
                s_.activation(ln_std[b][:], ln_var[b][:], AF.Sqrt)
                return s_.activation(acf[:], acf[:], AF.Copy)
            op("scalar", "ac", sqrt_f, waits=[f"lnH{tc}", "NOW:ac"], ev=f"lnI{tc}")

            op("vector", "ve", lambda v, b=b: v.reciprocal(ln_rstd[b][:], ln_std[b][:]),
               waits=[f"lnI{tc}"], ev=f"lnJ{tc}")

            def lnK(v, b=b):
                return v.tensor_scalar(ln_nb[b][:], ln_sm[b][:], ln_rstd[b][:], -1.0,
                                       OP.mult, OP.mult)
            op("vector", "ve", lnK, waits=[f"lnJ{tc}"], ev=f"lnK{tc}")

            wn = [f"lnK{tc}"]
            if tc >= 2:
                wn.append(f"p0tr_{tc - 2}")
            def lnL(v, b=b):
                return v.tensor_scalar(v_xntok[b][:], v_xt[b][:], ln_rstd[b][:], ln_nb[b][:],
                                       OP.mult, OP.add)
            op("vector", "ve", lnL, waits=wn, ev=f"lnL{tc}")

            if tc >= 1:
                p0_tr(tc - 1)
        p0_tr(7)

        for c in range(32):
            q = c // 8
            w = ["p0ev_7_3", "c_exg"] if c == 0 else []

            def xgmm(t, c=c, q=q):
                r = None
                for th in range(2):
                    r = t.matmul(p_xg[:, th * 512:(th + 1) * 512],
                                 e_xg[:, q * 32:(q + 1) * 32],
                                 xn[:, c * T + th * 512: c * T + (th + 1) * 512],
                                 start=(c == 0), stop=(c == 31), skip_group_check=True)
                return r
            op("tensor", "pe", xgmm, waits=w,
               ev="xgmm" if c == 31 else None)
        op("vector", "ve", lambda v: v.tensor_scalar(xgtn[:], p_xg[:], -1.0, None, OP.mult),
           waits=["xgmm"], ev="xgtn")
        if debug:
            dma("gpsimd", "gq", dbg["xg"], xgtn[:], waits=["xgtn"], ev="dbg_xg")
            dma("gpsimd", "gq", dbg["xn"], xn[:], waits=["p0ev_7_3"], ev="dbg_xn")

        def p1_loads(pc):
            qp, jp, rp0, rpw, c0 = PCS[pc]
            W = 2 * rpw
            b = pc % 2
            wl = [f"deq{pc - 2}"] if pc >= 2 else []
            dma("sync", "sq",
                qw_t[b][:].rearrange("p (q w) -> p q w", q=4)[:, :, 0:W],
                qwp[:, c0:c0 + W].rearrange("(q p) w -> p q w", q=4),
                waits=wl, ev=f"qwld{pc}")
            for q in range(4):
                dma("sync", "sq",
                    sc_t[b][:, q * 256:q * 256 + W],
                    scpe[q][:, c0:c0 + W],
                    ev=f"scld{pc}_{q}")
            dma("sync", "sq", sc_c[b][:, 0:W], scp[:, c0:c0 + W], ev=f"sccld{pc}")
            dma("sync", "sq", qz_t[b][:, 0:rpw], qzg[:, rp0:rp0 + rpw], ev=f"qzg{pc}")
            dma("sync", "sq", qz_t[b][:, 128:128 + rpw], qzg[:, RPD + rp0:RPD + rp0 + rpw],
                ev=f"qzu{pc}")

        def p1_deq(pc):
            qp, jp, rp0, rpw, c0 = PCS[pc]
            W = 2 * rpw
            b = pc % 2
            wd = [f"qwld{pc}", f"sccld{pc}", f"qzu{pc}", f"qzg{pc}"] \
                + [f"scld{pc}_{q}" for q in range(4)]
            if pc >= 2:
                wd.append(f"umm{pc - 2}")
            if pc == 0:
                wd += ["p0ev_7_3", "xgtn", "p0tr_7", "xgmm"]

            def deq(v, b=b, rpw=rpw, W=W, jp=jp):
                r = None
                for h in range(2):
                    for q in (2 * h, 2 * h + 1):
                        for s in range(4):
                            ti = (q % 2) * 4 + s
                            v.tensor_scalar(v_and[:, ti * 256:ti * 256 + W],
                                            qw_t[b][:, q * 256:q * 256 + W],
                                            ANDMASKS[s], None, OP.bitwise_and)
                    if h == 0:
                        v.tensor_scalar(zq_s[:, 0:rpw], qz_t[b][:, 0:rpw], 4 * jp, 15,
                                        OP.logical_shift_right, OP.bitwise_and)
                        v.tensor_scalar(zq_s[:, 128:128 + rpw], qz_t[b][:, 128:128 + rpw],
                                        4 * jp, 15, OP.logical_shift_right, OP.bitwise_and)
                    else:
                        v.scalar_tensor_tensor(szqm[b][:, 0:rpw], zq_s[:, 0:rpw], 1.0,
                                               sc_c[b][:, 0:rpw], OP.add, OP.mult)
                        v.scalar_tensor_tensor(szqm[b][:, 128:128 + rpw],
                                               zq_s[:, 128:128 + rpw], 1.0,
                                               sc_c[b][:, rpw:W], OP.add, OP.mult)
                    for q in (2 * h, 2 * h + 1):
                        for s in range(4):
                            ti = (q % 2) * 4 + s
                            t0 = q * 4 + s
                            r = v.scalar_tensor_tensor(
                                v_wdeq[b][:, t0 * 512:t0 * 512 + 2 * W].rearrange(
                                    "p (w two) -> p w two", two=2),
                                v_and[:, ti * 256:ti * 256 + W].bitcast(U16).rearrange(
                                    "p (w two) -> p w two", two=2),
                                2.0 ** (-4 * s),
                                sc_t[b][:, q * 256:q * 256 + W].unsqueeze(2).broadcast_to(
                                    [128, W, 2]),
                                OP.mult, OP.mult)
                return r
            op("vector", "ve", deq, waits=wd, ev=f"deq{pc}")

        p1_loads(0)
        p1_loads(1)
        p1_deq(0)

        for pc in range(NPC):
            qp, jp, rp0, rpw, c0 = PCS[pc]
            W = 2 * rpw
            b = pc % 2

            if pc < 64:
                mc, half = pc // 2, pc % 2
                wm = []
                if pc >= 1:
                    wm.append(f"mtree{pc - 1}")
                if pc >= 4:
                    wm.append(f"mcvt{2 * (mc - 2)}")
                dma("scalar", "aq", mask_t[:],
                    msk[mc * 128:(mc + 1) * 128, half * 2048:(half + 1) * 2048],
                    waits=wm, ev=f"mld{pc}")

                def mtree(g, mc=mc, half=half):
                    src = mask_t[:].rearrange("p (k c) -> p k c", k=64)
                    o1 = mtr.rearrange("p (k c) -> p k c", k=64)
                    o2 = mtr[:, 0:512].rearrange("p (k c) -> p k c", k=64)
                    o3 = mtr[:, 0:256].rearrange("p (k c) -> p k c", k=64)
                    o4 = mtr[:, 0:128].rearrange("p (k c) -> p k c", k=64)
                    g.tensor_tensor(o1, src[:, :, 0:16], src[:, :, 16:32], OP.add)
                    g.tensor_tensor(o2, o1[:, :, 0:8], o1[:, :, 8:16], OP.add)
                    g.tensor_tensor(o3, o2[:, :, 0:4], o2[:, :, 4:8], OP.add)
                    g.tensor_tensor(o4, o3[:, :, 0:2], o3[:, :, 2:4], OP.add)
                    dst = colsum[mc % 2][:, half * 64:(half + 1) * 64]
                    return g.tensor_tensor(dst.unsqueeze(2), o4[:, :, 0:1], o4[:, :, 1:2], OP.add)
                wtr_ = [f"mld{pc}"] + (["lnr7_3"] if pc < 2 else [])
                op("vector", "ve", mtree, waits=wtr_, ev=f"mtree{pc}")

                if half == 1:
                    wcv_ = [f"mtree{pc}"]
                    if mc >= 4:
                        wcv_.append(f"bmm{mc - 4}")
                    op("vector", "ve",
                       lambda v, mc=mc: v.tensor_copy(colsb[mc % 4][:], colsum[mc % 2][:]),
                       waits=wcv_, ev=f"mcvt{2 * mc}")

            if pc >= 6 and pc % 2 == 0 and (pc - 6) // 2 < 32:
                mcb = (pc - 6) // 2
                wem = [f"bmm{mcb - 2}"] if mcb >= 2 else []
                dma("scalar", "aq", e_mask[mcb % 2][:], em[mcb], waits=wem, ev=f"emld{mcb}")

                def bmm(t, mcb=mcb):
                    return t.matmul(p_bm[:, 0:128], e_mask[mcb % 2][:],
                                    colsb[mcb % 4][:], start=(mcb == 0), stop=(mcb == 31),
                                    skip_group_check=True)
                op("tensor", "pe", bmm, waits=[f"mcvt{2 * mcb}", f"emld{mcb}"], ev=f"bmm{mcb}")

            wg = [f"deq{pc}"]
            if pc >= 1:
                wg.append(f"silu{pc - 1}")

            def gmm(t, b=b, rpw=rpw, W=W):
                r = None
                for q in range(4):
                    for s in range(4):
                        for sel in range(2):
                            c = q * 8 + s + 4 * sel
                            t0 = q * 4 + s
                            lhs = v_wdeq[b][:, t0 * 512:t0 * 512 + 2 * W].rearrange(
                                "p (w two) -> p w two", two=2)[:, 0:rpw, sel]
                            for th in range(2):
                                t.matmul(p_gate[0:rpw, th * 512:(th + 1) * 512], lhs,
                                         xn[:, c * T + th * 512: c * T + (th + 1) * 512],
                                         start=(q == 0 and s == 0 and sel == 0),
                                         stop=False, skip_group_check=True)
                for th in range(2):
                    r = t.matmul(p_gate[0:rpw, th * 512:(th + 1) * 512],
                                 szqm[b][0:32, 0:rpw],
                                 xgtn[0:32, th * 512:(th + 1) * 512],
                                 start=False, stop=True, skip_group_check=True)
                return r
            op("tensor", "pe", gmm, waits=wg, ev=f"gmm{pc}")

            if pc >= 1:
                qpm, jpm, rp0m, rpwm, _ = PCS[pc - 1]

                def hgmm(t, pcm=pc - 1, qpm=qpm, rpwm=rpwm):
                    r = None
                    for th in range(2):
                        r = t.matmul(p_hg[:, th * 512:(th + 1) * 512],
                                     e_hg[0:rpwm, qpm * 86:(qpm + 1) * 86],
                                     a_hact[pcm % 2][0:rpwm, th * 512:(th + 1) * 512],
                                     start=(pcm == 0), stop=False, skip_group_check=True)
                    return r
                op("tensor", "pe", hgmm, waits=[f"hact{pc - 1}", "c_ehg"],
                   ev=f"hgmm{pc - 1}")

            wu = []
            if pc >= 1:
                wu.append(f"hact{pc - 1}")

            def umm(t, b=b, rpw=rpw, W=W):
                r = None
                for q in range(4):
                    for s in range(4):
                        for sel in range(2):
                            c = q * 8 + s + 4 * sel
                            t0 = q * 4 + s
                            lhs = v_wdeq[b][:, t0 * 512:t0 * 512 + 2 * W].rearrange(
                                "p (w two) -> p w two", two=2)[:, rpw:W, sel]
                            for th in range(2):
                                t.matmul(p_up[0:rpw, th * 512:(th + 1) * 512], lhs,
                                         xn[:, c * T + th * 512: c * T + (th + 1) * 512],
                                         start=(q == 0 and s == 0 and sel == 0),
                                         stop=False, skip_group_check=True)
                for th in range(2):
                    r = t.matmul(p_up[0:rpw, th * 512:(th + 1) * 512],
                                 szqm[b][0:32, 128:128 + rpw],
                                 xgtn[0:32, th * 512:(th + 1) * 512],
                                 start=False, stop=True, skip_group_check=True)
                return r
            op("tensor", "pe", umm, waits=wu, ev=f"umm{pc}")

            ws = [f"gmm{pc}"]
            if pc >= 1:
                ws.append(f"hact{pc - 1}")
            def silu_f(s_, rpw=rpw):
                s_.activation(a_silu[0:rpw, :], p_gate[0:rpw, :], AF.Silu)
                return s_.activation(acf[:], acf[:], AF.Copy)
            op("scalar", "ac", silu_f, waits=ws + ["NOW:ac"], ev=f"silu{pc}")

            if pc + 1 < NPC:
                p1_deq(pc + 1)

            wh = [f"umm{pc}", f"silu{pc}"]
            if pc >= 2:
                wh += [f"hsc{pc - 2}", f"hgmm{pc - 2}"]

            def hev(v, pc=pc, rpw=rpw):
                return v.tensor_tensor(a_hact[pc % 2][0:rpw, :], a_silu[0:rpw, :],
                                       p_up[0:rpw, :], OP.mult)
            op("vector", "ve", hev, waits=wh, ev=f"hact{pc}")

            if pc + 2 < NPC:
                p1_loads(pc + 2)

            pos = JPOS[jp]
            dma("gpsimd", "gq", hsc[qp][:, pos * T:(pos + 1) * T], a_hact[pc % 2][:],
                waits=[f"hact{pc}"], ev=f"hsc{pc}")
            if debug and pc == 0:
                dma("gpsimd", "gq", dbg["h0"], a_hact[0][:], waits=[f"hact{pc}"], ev="dbg_h0")

        qpm, jpm, rp0m, rpwm, _ = PCS[NPC - 1]

        def hgl(t, qpm=qpm, rpwm=rpwm):
            r = None
            for th in range(2):
                r = t.matmul(p_hg[:, th * 512:(th + 1) * 512],
                             e_hg[0:rpwm, qpm * 86:(qpm + 1) * 86],
                             a_hact[(NPC - 1) % 2][0:rpwm, th * 512:(th + 1) * 512],
                             start=False, stop=True, skip_group_check=True)
            return r
        op("tensor", "pe", hgl, waits=[f"hact{NPC - 1}"], ev=f"hgmm{NPC - 1}")
        op("vector", "ve", lambda v: v.tensor_scalar(hgn[:], p_hg[:], -1.0, None, OP.mult),
           waits=[f"hgmm{NPC - 1}"], ev="hgn")

        op("vector", "ve", lambda v: v.tensor_scalar(bm_sb[:], p_bm[:, 0:128], 0.0, None, OP.is_gt),
           waits=["bmm31"], ev="bmsb")
        op("tensor", "pe", lambda t: t.transpose(p_bmt[:, 0:128], bm_sb[:], id_b[:]),
           waits=["bmsb", "c_idb"], ev="bmtr")
        op("vector", "ve", lambda v: v.tensor_copy(bmt_sb[:], p_bmt[:, 0:128]),
           waits=["bmtr"], ev="bmtev")
        for q in range(4):
            def bx(t, q=q):
                return t.matmul(p_bmxf[:, 0:128], e_mq[:, q * 128:(q + 1) * 128],
                                bmt_sb[:], start=True, stop=True, skip_group_check=True)
            wq_ = (["bmtev"] if q == 0 else [f"bmxe{q - 1}"]) + ["c_emq"]
            op("tensor", "pe", bx, waits=wq_, ev=f"bmx{q}")
            op("vector", "ve",
               lambda v, q=q: v.tensor_copy(bmx[:, q * 128:(q + 1) * 128], p_bmxf[:, 0:128]),
               waits=[f"bmx{q}"], ev=f"bmxe{q}")
        if debug:
            dma("gpsimd", "gq", dbg["bm"], bm_sb[:], waits=["bmsb"], ev="dbg_bm")
            dma("gpsimd", "gq", dbg["hg"], hgn[:], waits=["hgn"], ev="dbg_hg")

        for sp in range(NSUP):
            H0 = 512 * sp
            wzl = [f"szqd{sp - 1}", f"o2{sp - 1}"] if sp >= 1 else ["mtree63", "bmm31"]
            dma("sync", "sq", qzd_t, qzd[:, 64 * sp:64 * (sp + 1)], waits=wzl,
                ev=f"qzdld{sp}")
            dma("sync", "sq", scd_n, scd[:, H0:H0 + 512], ev=f"scdnld{sp}")

            def zunA(v):
                r = None
                for j in range(8):
                    r = v.tensor_scalar(
                        zqd_s.rearrange("p (w e) -> p w e", e=8)[:, :, j].unsqueeze(2),
                        qzd_t.unsqueeze(2), 4 * j, 15,
                        OP.logical_shift_right, OP.bitwise_and)
                return r
            op("vector", "ve", zunA, waits=[f"qzdld{sp}", f"scdnld{sp}"], ev=f"zunA{sp}")

            def zunB(v):
                return v.scalar_tensor_tensor(szqd[:], zqd_s, 1.0, scd_n, OP.add, OP.mult)
            op("vector", "ve", zunB, waits=[f"zunA{sp}"], ev=f"szqd{sp}")
            if debug and sp == 1:
                dma("gpsimd", "gq", dbg["szq"], szqd[:], waits=["szqd1"], ev="dbg_szq")

            def dn_deq(qi):
                rp0 = 128 * qi
                rpw = min(128, RPD - rp0)
                b3 = qi % 2
                wq_ = []
                if qi >= 2:
                    wq_.append(f"ddB{sp}_{qi - 2}_3")
                elif sp >= 1:
                    wq_.append(f"ddB{sp - 1}_{10 - qi}_3")
                dma("scalar", "aq", qwd_t[b3][0:rpw, :], qwd[rp0:rp0 + rpw, H0:H0 + 512],
                    waits=wq_, ev=f"qwdld{sp}_{qi}")
                dma("scalar", "aq", scd_e[b3][:],
                    scde[qi][:, H0:H0 + 512],
                    ev=f"scdld{sp}_{qi}")
                for s4 in range(4):
                    bs = (qi % 2) * 4 + s4
                    wv = [f"qwdld{sp}_{qi}", f"scdld{sp}_{qi}"]
                    if sp == 0 and qi == 0 and s4 == 0:
                        wv.append("bmxe3")
                    if qi >= 2:
                        wv.append(f"dnm{sp}_{qi - 2}_{s4}")
                    elif sp >= 1:
                        wv.append(f"dnm{sp - 1}_{9 + qi}_{s4}")

                    def ddA(v, s=s4, b3=b3):
                        return v.tensor_scalar(and_d[s % 2][:], qwd_t[b3][:], ANDMASKS[s], None,
                                               OP.bitwise_and)
                    op("vector", "ve", ddA, waits=wv, ev=f"ddA{sp}_{qi}_{s4}")

                    def ddB(v, s=s4, bs=bs, b3=b3):
                        wd_ = wdeq_d[bs][:] if bs < 2 else wdeq_d[bs]
                        return v.scalar_tensor_tensor(
                            wd_.rearrange("p (w two) -> p w two", two=2),
                            and_d[s % 2][:].bitcast(U16).rearrange("p (w two) -> p w two", two=2),
                            2.0 ** (-4 * s),
                            scd_e[b3][:].unsqueeze(2).broadcast_to([128, 512, 2]),
                            OP.mult, OP.mult)
                    op("vector", "ve", ddB, waits=[f"ddA{sp}_{qi}_{s4}"], ev=f"ddB{sp}_{qi}_{s4}")

            first_dn = True
            if sp == 0:
                dn_deq(0)
                dn_deq(1)
            for qi in range(11):
                rp0 = 128 * qi
                rpw = min(128, RPD - rp0)
                for s in range(4):
                    bs = (qi % 2) * 4 + s
                    wh_ = []
                    if qi >= 1 or s >= 2:
                        pqi, pss = (s - 2, qi) if s >= 2 else (s + 2, qi - 1)
                        wh_.append(f"dnm{sp}_{pss}_{pqi}")
                    elif sp >= 1:
                        wh_.append(f"dnm{sp - 1}_10_{s + 2}")
                    q_, qs_ = (("gpsimd", "gq") if (4 * qi + s) % 2 == 0 else ("scalar", "aq"))
                    dma(q_, qs_, hin[s % 2][:],
                        hsc[qi][:, 2 * s * T:(2 * s + 2) * T],
                        waits=wh_, ev=f"hinld{sp}_{qi}_{s}")

                    wmm = [f"ddB{sp}_{qi}_{s}", f"hinld{sp}_{qi}_{s}"]
                    if first_dn:
                        if sp >= 1:
                            wmm += [f"gsb{sp - 1}", f"oev{sp - 1}_7"]
                        else:
                            wmm += ["hgn", "bmxe3"]

                    def dmm(t, s=s, bs=bs, rpw=rpw, first=first_dn):
                        r = None
                        for jj in range(2):
                            for hc in range(4):
                                wd_ = wdeq_d[bs][:] if bs < 2 else wdeq_d[bs]
                                lhs = wd_.rearrange("p (w two) -> p w two", two=2)[
                                    0:rpw, hc * 128:(hc + 1) * 128, jj]
                                for th in range(2):
                                    r = t.matmul(
                                        p_o2[hc][0:128, th * 512:(th + 1) * 512], lhs,
                                        hin[s % 2][0:rpw, jj * T + th * 512: jj * T + (th + 1) * 512],
                                        start=(first and jj == 0),
                                        stop=False, skip_group_check=True)
                        return r
                    op("tensor", "pe", dmm, waits=wmm, ev=f"dnm{sp}_{qi}_{s}")
                    first_dn = False
                if qi + 2 < 11:
                    dn_deq(qi + 2)

            def dz(t):
                r = None
                for hc in range(4):
                    for th in range(2):
                        r = t.matmul(p_o2[hc][0:128, th * 512:(th + 1) * 512],
                                     szqd[0:86, hc * 128:(hc + 1) * 128],
                                     hgn[0:86, th * 512:(th + 1) * 512],
                                     start=False, stop=True, skip_group_check=True)
                return r
            wdz = [f"szqd{sp}", f"dnm{sp}_10_3"]
            if debug and sp == 1:
                wdz.append("dbg_szq")
            op("tensor", "pe", dz, waits=wdz, ev=f"o2{sp}")

            def gslA(s_):
                for hc in (0, 3):
                    s_.activation(g_sb[:, hc * T:(hc + 1) * T], p_o2[hc][:], AF.Silu)
                return s_.activation(acf[:], acf[:], AF.Copy)

            def gslB(s_):
                for hc in (1, 2):
                    s_.activation(g_sb[:, hc * T:(hc + 1) * T], p_o2[hc][:], AF.Silu)
                return s_.activation(acf[:], acf[:], AF.Copy)
            wgs = [f"o2{sp}"]
            if debug and sp == 1:
                wgs.append("dbg_g0")
            if debug and sp == 2:
                wgs.append("dbg_g1")
            op("scalar", "ac", gslA, waits=wgs + ["NOW:ac"], ev=f"gsbA{sp}")
            op("scalar", "ac", gslB, waits=["NOW:ac"], ev=f"gsb{sp}")
            if debug and sp == 0:
                dma("gpsimd", "gq", dbg["g0"], g_sb[:, 0:T], waits=["gsb0"], ev="dbg_g0")
            if debug and sp == 1:
                dma("gpsimd", "gq", dbg["g1"], g_sb[:, 0:T], waits=["gsb1"], ev="dbg_g1")

            def attn_prep(ncl):
                gc = 4 * sp + ncl
                for h2 in range(2):
                    ww = []
                    if sp == 0 and ncl == 0:
                        ww = [f"umm{NPC - 1}"]
                    if ncl >= 1:
                        ww.append(f"wcv{(sp, ncl - 1, h2)}")
                    elif sp >= 1:
                        ww.append(f"wcv{(sp - 1, 3, h2)}")
                    dma("sync", "sq", v_wrow[:, h2 * 2048:(h2 + 1) * 2048],
                        wp[gc * 128:(gc + 1) * 128, h2 * 2048:(h2 + 1) * 2048],
                        waits=ww, ev=f"wld{(sp, ncl, h2)}")
                    wc = [f"wld{(sp, ncl, h2)}"]
                    if ncl >= 1:
                        wc.append(f"str{(sp, ncl - 1, 7)}")
                    elif sp >= 1:
                        wc.append(f"str{(sp - 1, 3, 7)}")

                    def wcv_f(s_, h2=h2):
                        s_.activation(v_wb[:, h2 * 2048:(h2 + 1) * 2048],
                                      v_wrow[:, h2 * 2048:(h2 + 1) * 2048], AF.Copy)
                        return s_.activation(acf[:], acf[:], AF.Copy)
                    op("scalar", "ac", wcv_f, waits=wc + ["NOW:ac"], ev=f"wcv{(sp, ncl, h2)}")

                for bt in range(8):
                    par = bt % 4
                    wt_ = [f"wcv{(sp, ncl, 0 if bt < 4 else 1)}"]
                    if bt < 4 and ncl == 0:
                        wt_.append(f"gsbA{sp}")
                    if bt >= 4:
                        wt_.append(f"wte{(sp, ncl, bt - 4)}")
                    elif ncl >= 1:
                        wt_.append(f"wte{(sp, ncl - 1, bt + 4)}")
                    elif sp >= 1:
                        wt_.append(f"wte{(sp - 1, 3, bt + 4)}")

                    def wtr(t, bt=bt, par=par):
                        r = None
                        for i in range(4):
                            c = bt * 4 + i
                            r = t.matmul(p_wt[par][:, i * 128:(i + 1) * 128],
                                         v_wb[:, c * 128:(c + 1) * 128], id_b[:],
                                         start=True, stop=True, skip_group_check=True)
                        return r
                    op("tensor", "pe", wtr, waits=wt_, ev=f"str{(sp, ncl, bt)}")

                    def wte(v, bt=bt, par=par, gc=gc):
                        qb = bt // 2
                        m_ = bmx[:, qb * 128 + 4 * gc: qb * 128 + 4 * gc + 4]
                        return v.tensor_tensor(
                            v_wtb[:, bt * 512:(bt + 1) * 512].rearrange(
                                "p (k a e) -> p k a e", k=4, a=4),
                            p_wt[par][:].rearrange("p (k a e) -> p k a e", k=4, a=4),
                            m_.unsqueeze(1).unsqueeze(3).broadcast_to([128, 4, 4, 32]),
                            OP.mult)
                    op("vector", "ve", wte, waits=[f"str{(sp, ncl, bt)}"],
                       ev=f"wte{(sp, ncl, bt)}")

            def attn_cmb(ncl):
                def cmb(v, ncl=ncl):
                    return v.tensor_tensor(v_comb[:, ncl * T:(ncl + 1) * T],
                                           p_sp[ncl % 2][:, 0:1024],
                                           g_sb[:, ncl * T:(ncl + 1) * T], OP.mult)
                wcmb = [f"smm{sp}_{ncl}", f"gsb{sp}"]
                if debug and sp == 2 and ncl == 0:
                    wcmb.append("dbg_c1")
                op("vector", "ve", cmb, waits=wcmb, ev=f"comb{sp}_{ncl}")
                if debug and sp == 1 and ncl == 0:
                    dma("gpsimd", "gq", dbg["c1"], v_comb[:, 0:T], waits=["comb1_0"], ev="dbg_c1")

            def attn_smm(ncl):
                wa = [f"wte{(sp, ncl, 7)}"]
                if ncl < 2:
                    wa.append(f"gsb{sp}")
                if ncl >= 2:
                    wa.append(f"comb{sp}_{ncl - 2}")
                elif sp >= 1:
                    wa.append(f"comb{sp - 1}_{ncl + 2}")
                    if ncl == 1:
                        wa.append(f"oev{sp - 1}_7")
                else:
                    wa.append(f"gsb{sp}")
                    if ncl == 0:
                        wa.append(f"hsc{NPC - 1}")

                def smm(t, ncl=ncl):
                    r = None
                    for c in range(32):
                        for th in range(2):
                            r = t.matmul(p_sp[ncl % 2][:, th * 512:(th + 1) * 512],
                                         v_wtb[:, c * 128:(c + 1) * 128],
                                         xn[:, c * T + th * 512: c * T + (th + 1) * 512],
                                         start=(c == 0), stop=(c == 31),
                                         skip_group_check=True)
                    return r
                op("tensor", "pe", smm, waits=wa, ev=f"smm{sp}_{ncl}")

            if sp + 1 < NSUP:
                sp_s, H0_s = sp, H0
                sp, H0 = sp + 1, 512 * (sp + 1)
                dn_deq(0)
                dn_deq(1)
                sp, H0 = sp_s, H0_s
            for ncl in range(4):
                attn_prep(ncl)
                if ncl >= 1:
                    attn_cmb(ncl - 1)
                attn_smm(ncl)
            attn_cmb(3)

            for tc in range(8):
                pto = tc % 2
                b4 = tc % 2
                wo = [f"comb{sp}_3", f"comb{sp}_2", f"comb{sp}_1", f"comb{sp}_0", "c_idf"]
                if tc >= 2:
                    wo.append(f"oev{sp}_{tc - 2}")

                def otr(t, tc=tc, pto=pto):
                    r = None
                    for ncl in range(4):
                        r = t.transpose(p_ot[pto][:, ncl * 128:(ncl + 1) * 128],
                                        v_comb[:, ncl * T + tc * 128: ncl * T + (tc + 1) * 128],
                                        id_f[:])
                    return r
                op("tensor", "pe", otr, waits=wo, ev=f"otr{sp}_{tc}")

                wx = [f"oev{sp}_{tc - 2}"] if tc >= 2 else \
                    ([f"ost{sp - 1}_{tc + 6}"] if sp >= 1 else ["mtree63"])
                dma("sync", "sq", x2[b4], xr[tc * 128:(tc + 1) * 128, H0:H0 + 512],
                    waits=wx, ev=f"xr{sp}_{tc}")

                wv_ = [f"otr{sp}_{tc}", f"xr{sp}_{tc}"]
                if tc >= 2:
                    wv_.append(f"ost{sp}_{tc - 2}")
                elif sp >= 1:
                    wv_.append(f"ost{sp - 1}_{tc + 6}")

                def oev(v, pto=pto, b4=b4):
                    return v.tensor_tensor(osb[b4][:], p_ot[pto][:], x2[b4], OP.add)
                op("vector", "ve", oev, waits=wv_, ev=f"oev{sp}_{tc}")

                dma("gpsimd", "gq", out[tc * 128:(tc + 1) * 128, H0:H0 + 512], osb[b4][:],
                    waits=[f"oev{sp}_{tc}"], ev=f"ost{sp}_{tc}")

        S.semh = sems
        S.verify_no_deadlock()

        with nc.Block() as block:
            def emit(eng_name):
                def body(eng):
                    waited = {}
                    for fn, waits, ev, sem, amt in S.ops[eng_name]:
                        for w in waits:
                            s_, v_ = S.events[w]
                            if waited.get(s_, -1) < v_:
                                eng.wait_ge(sems[s_], v_)
                                waited[s_] = v_
                        instr = fn(eng)
                        if ev is not None:
                            assert instr is not None, f"op for ev {ev} returned no instr"
                            instr.then_inc(sems[sem], amt)
                    if eng_name == "gpsimd":
                        eng.wait_ge(sems["gq"], S.count["gq"])
                return body
            block.sync(emit("sync"))
            block.vector(emit("vector"))
            block.scalar(emit("scalar"))
            block.tensor(emit("tensor"))
            block.gpsimd(emit("gpsimd"))

    return nc


def _consts():
    em = np.zeros((32, 128, 128), dtype=BF)
    for c in range(32):
        for p in range(128):
            em[c, p, 4 * c + p // 32] = 1
    emq = np.zeros((4, 128, 128), dtype=BF)
    for q in range(4):
        for p in range(128):
            emq[q, 32 * q + p // 4, p] = 1
    exg = np.zeros((4, 128, 32), dtype=BF)
    for q in range(4):
        for p in range(128):
            exg[q, p, 8 * q + p // 16] = 1
    ehg = np.zeros((11, 128, 86), dtype=BF)
    for qp in range(11):
        rpw = min(128, RPD - 128 * qp)
        for p in range(rpw):
            ehg[qp, p, 8 * qp + p // 16] = 1
    idb = np.eye(128, dtype=BF)
    idf = np.eye(128, dtype=np.float32)
    return {"em": em, "emq": emq, "exg": exg, "ehg": ehg, "idb": idb, "idf": idf}


_NC_CACHE = {}


def kernel(L0_attn_data, L0_attn_weight, L0_attn_mask,
           L0_ffn_qweight_gu, L0_ffn_qzeros_gu, L0_ffn_scales_gu, L0_ffn_g_idx_gu,
           L0_ffn_qweight_down, L0_ffn_qzeros_down, L0_ffn_scales_down, L0_ffn_g_idx_down,
           _debug=False, _core_ids=None):
    x = np.ascontiguousarray(np.asarray(L0_attn_data, dtype=np.float32).reshape(8192, D))
    w = np.asarray(L0_attn_weight, dtype=np.float32)
    mask = np.asarray(L0_attn_mask, dtype=np.int32)
    qw_gu = np.asarray(L0_ffn_qweight_gu, dtype=np.uint32)
    qz_gu = np.asarray(L0_ffn_qzeros_gu, dtype=np.uint32)
    sc_gu = np.asarray(L0_ffn_scales_gu, dtype=np.float32)
    qw_d = np.asarray(L0_ffn_qweight_down, dtype=np.uint32)
    qz_d = np.asarray(L0_ffn_qzeros_down, dtype=np.uint32)
    sc_d = np.asarray(L0_ffn_scales_down, dtype=np.float32)

    wperm = np.ascontiguousarray(w[:, PI])
    qwperm = np.ascontiguousarray(qw_gu[:, COLPERM])
    scperm = np.ascontiguousarray(sc_gu[:, COLPERM])
    scpe = np.ascontiguousarray(
        np.stack([np.repeat(scperm[8 * q:8 * q + 8], 16, axis=0) for q in range(4)]))
    scde = np.zeros((11, 128, D), dtype=np.float32)
    for qi_ in range(11):
        ng = min(8, 86 - 8 * qi_)
        scde[qi_, 0:16 * ng] = np.repeat(sc_d[8 * qi_:8 * qi_ + ng], 16, axis=0)
    scde = np.ascontiguousarray(scde)
    consts = _consts()

    import os
    p2s = int(os.environ.get("P2_SERIAL", "0"))
    key = ("prog", bool(_debug), p2s)
    if key not in _NC_CACHE:
        _NC_CACHE[key] = build_program(debug=_debug, p2_serial=p2s)
    nc = _NC_CACHE[key]

    core_ids = _core_ids if _core_ids is not None else list(range(NCORES))
    in_maps = []
    for c in core_ids:
        sl = x[c * T:(c + 1) * T]
        m = {
            "xp": np.ascontiguousarray(sl[:, PI]),
            "xr": np.ascontiguousarray(sl),
            "wp": wperm, "msk": mask, "qwp": qwperm, "scp": scperm,
            "scpe": scpe, "scde": scde,
            "qzg": qz_gu, "qwd": qw_d, "qzd": qz_d, "scd": sc_d,
        }
        m.update(consts)
        in_maps.append(m)

    res = run_bass_kernel_spmd(nc, in_maps, core_ids=core_ids)
    outs = [r["out"] for r in res.results]
    if _debug:
        kernel._debug_results = res.results
    full = np.concatenate(outs, axis=0)
    if len(core_ids) < NCORES:
        return full
    return full.reshape(4, 2048, D)


# version 20
# speedup vs baseline: 1.1751x; 1.1751x over previous
import contextlib
import numpy as np
import ml_dtypes
import concourse.bass as bass
import concourse.mybir as mybir
from concourse.bass_utils import run_bass_kernel_spmd
from concourse.alu_op_type import AluOpType as OP

F32 = mybir.dt.float32
BF16 = mybir.dt.bfloat16
U32 = mybir.dt.uint32
U16 = mybir.dt.uint16
I32 = mybir.dt.int32
AF = mybir.ActivationFunctionType
AX = mybir.AxisListType.X
BF = ml_dtypes.bfloat16

NCORES = 8
T = 1024
D = 4096
INTER = 11008
RPD = 1376
NPC = 88
NSUP = 8

PCS = []
_c0 = 0
for _qp in range(11):
    _rp0 = 128 * _qp
    _rpw = min(128, RPD - _rp0)
    for _jp in range(8):
        PCS.append((_qp, _jp, _rp0, _rpw, _c0))
        _c0 += 2 * _rpw
assert _c0 == 2 * INTER

JLIST = [0, 4, 1, 5, 2, 6, 3, 7]
JPOS = {j: k for k, j in enumerate(JLIST)}

PI = np.zeros(D, dtype=np.int64)
for _q in range(4):
    for _j in range(8):
        _c = _q * 8 + _j
        PI[_c * 128:(_c + 1) * 128] = 8 * (128 * _q + np.arange(128)) + _j

COLPERM = np.zeros(2 * INTER, dtype=np.int64)
for (_qp, _jp, _rp0, _rpw, _cc) in PCS:
    rps = _rp0 + np.arange(_rpw)
    COLPERM[_cc:_cc + _rpw] = 8 * rps + _jp
    COLPERM[_cc + _rpw:_cc + 2 * _rpw] = INTER + 8 * rps + _jp

ANDMASKS = [(0x000F000F << (4 * s)) & 0xFFFFFFFF for s in range(4)]


class Sched:
    ENGS = ("sync", "vector", "scalar", "tensor", "gpsimd")

    sems = ("sq", "gq", "aq", "ve", "ac", "pe", "gp", "sqa", "gqa", "aqa")

    ACKQ = {"sq": ("sync", "sqa"), "gq": ("gpsimd", "gqa"), "aq": ("scalar", "aqa")}

    def __init__(self):
        self.ops = {e: [] for e in self.ENGS}
        self.count = {s: 0 for s in self.sems}
        self.events = {}
        self.acks = {"sq": [], "gq": [], "aq": []}
        self.semh = None

    def _ack_for(self, sm, val):
        for c, nm in self.acks[sm]:
            if c >= val:
                return nm
        eng, asem = self.ACKQ[sm]
        cur = self.count[sm]
        nm = f"__ack_{sm}_{cur}"

        def fn(e, _sm=sm, _cur=cur):
            return e.wait_ge(self.semh[_sm], _cur)
        self.count[asem] += 1
        self.events[nm] = (asem, self.count[asem])
        self.ops[eng].append((fn, (), nm, asem, 1))
        self.acks[sm].append((cur, nm))
        return nm

    def add(self, eng, fn, waits=(), ev=None, sem=None, amt=1):
        rw = []
        for w in waits:
            if isinstance(w, str) and w.startswith("NOW:"):
                sm = w[4:]
                nm = f"__now_{sm}_{len(self.ops[eng])}_{eng}_{self.count[sm]}"
                self.events[nm] = (sm, self.count[sm])
                rw.append(nm)
            else:
                assert w in self.events, f"wait on undefined event {w} (eng {eng})"
                sm, val = self.events[w]
                if sm in ("sq", "gq", "aq"):
                    rw.append(self._ack_for(sm, val))
                else:
                    rw.append(w)
        if ev is not None:
            assert sem is not None
            self.count[sem] += amt
            self.events[ev] = (sem, self.count[sem])
        self.ops[eng].append((fn, tuple(rw), ev, sem, amt))

    def verify_no_deadlock(self):
        pos = {e: 0 for e in self.ENGS}
        cnt = {s: 0 for s in self.sems}
        progress = True
        while progress:
            progress = False
            for e in self.ENGS:
                while pos[e] < len(self.ops[e]):
                    fn, waits, ev, sem, amt = self.ops[e][pos[e]]
                    if all(cnt[self.events[w][0]] >= self.events[w][1] for w in waits):
                        if ev is not None:
                            cnt[sem] += amt
                        pos[e] += 1
                        progress = True
                    else:
                        break
        stuck = {e: (pos[e], len(self.ops[e])) for e in self.ENGS if pos[e] < len(self.ops[e])}
        if stuck:
            detail = {}
            for e in stuck:
                fn, waits, ev, sem, amt = self.ops[e][pos[e]]
                detail[e] = [(w, self.events[w]) for w in waits
                             if cnt[self.events[w][0]] < self.events[w][1]]
            raise RuntimeError(f"schedule deadlock: stuck={stuck} detail={detail}")


def build_program(debug=False, p2_serial=0):
    nc = bass.Bass("TRN2", target_bir_lowering=False)

    xp = nc.dram_tensor("xp", [T, D], F32, kind="ExternalInput").ap()
    xr = nc.dram_tensor("xr", [T, D], F32, kind="ExternalInput").ap()
    wp = nc.dram_tensor("wp", [D, D], F32, kind="ExternalInput").ap()
    msk = nc.dram_tensor("msk", [D, D], I32, kind="ExternalInput").ap()
    qwp = nc.dram_tensor("qwp", [512, 2 * INTER], U32, kind="ExternalInput").ap()
    scp = nc.dram_tensor("scp", [32, 2 * INTER], F32, kind="ExternalInput").ap()
    scpe = nc.dram_tensor("scpe", [4, 128, 2 * INTER], F32, kind="ExternalInput").ap()
    qzg = nc.dram_tensor("qzg", [32, 2752], U32, kind="ExternalInput").ap()
    qwd = nc.dram_tensor("qwd", [RPD, D], U32, kind="ExternalInput").ap()
    qzd = nc.dram_tensor("qzd", [86, 512], U32, kind="ExternalInput").ap()
    scd = nc.dram_tensor("scd", [86, D], F32, kind="ExternalInput").ap()
    scde = nc.dram_tensor("scde", [11, 128, D], F32, kind="ExternalInput").ap()
    em = nc.dram_tensor("em", [32, 128, 128], BF16, kind="ExternalInput").ap()
    emq = nc.dram_tensor("emq", [4, 128, 128], BF16, kind="ExternalInput").ap()
    exg = nc.dram_tensor("exg", [4, 128, 32], BF16, kind="ExternalInput").ap()
    ehg = nc.dram_tensor("ehg", [11, 128, 86], BF16, kind="ExternalInput").ap()
    idb = nc.dram_tensor("idb", [128, 128], BF16, kind="ExternalInput").ap()
    idf = nc.dram_tensor("idf", [128, 128], F32, kind="ExternalInput").ap()
    out = nc.dram_tensor("out", [T, D], F32, kind="ExternalOutput").ap()
    hsc = nc.dram_tensor("hsc", [11, 128, 8 * T], BF16, kind="Internal").ap()
    dbg = {}
    if debug:
        dbg["xn"] = nc.dram_tensor("d_xn", [128, 32 * T], BF16, kind="ExternalOutput").ap()
        dbg["xg"] = nc.dram_tensor("d_xg", [32, T], BF16, kind="ExternalOutput").ap()
        dbg["hg"] = nc.dram_tensor("d_hg", [86, T], BF16, kind="ExternalOutput").ap()
        dbg["bm"] = nc.dram_tensor("d_bm", [128, 128], BF16, kind="ExternalOutput").ap()
        dbg["h0"] = nc.dram_tensor("d_h0", [128, T], BF16, kind="ExternalOutput").ap()
        dbg["g0"] = nc.dram_tensor("d_g0", [128, T], F32, kind="ExternalOutput").ap()
        dbg["c0"] = nc.dram_tensor("d_c0", [128, T], F32, kind="ExternalOutput").ap()
        dbg["g1"] = nc.dram_tensor("d_g1", [128, T], F32, kind="ExternalOutput").ap()
        dbg["hin"] = nc.dram_tensor("d_hin", [128, 2 * T], BF16, kind="ExternalOutput").ap()
        dbg["wdd"] = nc.dram_tensor("d_wdd", [128, 1024], BF16, kind="ExternalOutput").ap()
        dbg["szq"] = nc.dram_tensor("d_szq", [86, 512], BF16, kind="ExternalOutput").ap()
        dbg["c1"] = nc.dram_tensor("d_c1", [128, T], F32, kind="ExternalOutput").ap()

    with contextlib.ExitStack() as ctx:
        def sbuf(name, shape, dt):
            return ctx.enter_context(nc.sbuf_tensor(name, shape, dt))

        arena = sbuf("arena", [128, 49152], mybir.dt.uint8)
        _DTB = {F32: 4, BF16: 2, U32: 4, U16: 2, I32: 4}

        def ar(off, dt, cols):
            return arena[:, off:off + cols * _DTB[dt]].bitcast(dt)

        v_xt = [ar(0, F32, 4096), ar(16384, F32, 4096)]
        v_xntok = [ar(32768, BF16, 4096), ar(40960, BF16, 4096)]
        v_wdeq = [ar(0, BF16, 8192), ar(16384, BF16, 8192)]
        v_and = ar(32768, U32, 2048)
        a_silu = ar(40960, F32, 1024)
        a_hact = [ar(45056, BF16, 1024), ar(47104, BF16, 1024)]
        v_wrow = ar(0, F32, 2048)
        a_hin = ar(8192, BF16, 4096)
        v_wb = ar(16384, BF16, 4096)
        v_wtb = ar(24576, BF16, 4096)
        v_comb = ar(32768, F32, 4096)

        xn = sbuf("xn", [128, 32 * T], BF16)
        qw_t = [sbuf(f"qw{b}", [128, 1024], U32) for b in range(2)]
        sc_t = [sbuf(f"sct{b}", [128, 1024], F32) for b in range(2)]
        sc_c = [sbuf(f"scc{b}", [32, 512], F32) for b in range(2)]
        qz_t = [sbuf(f"qzt{b}", [32, 256], U32) for b in range(2)]
        zq_s = sbuf("zqs", [32, 256], U32)
        szqm = [sbuf(f"szq{b}", [32, 256], BF16) for b in range(2)]
        xgtn = sbuf("xgtn", [32, T], BF16)
        hgn = sbuf("hgn", [86, T], BF16)
        mask_t = sbuf("mskt", [128, 2048], I32)
        colsum = [sbuf(f"cs{b}", [128, 128], F32) for b in range(2)]
        colsb = [sbuf(f"csb{b}", [128, 128], BF16) for b in range(4)]
        bm_sb = sbuf("bm", [128, 128], BF16)
        bmt_sb = sbuf("bmt", [128, 128], BF16)
        bmx = sbuf("bmx", [128, 512], BF16)
        hin = [sbuf(f"hin{b}", [128, 2 * T], BF16) for b in range(2)]
        hin.append(a_hin[:, 0:2048])
        hin.append(a_hin[:, 2048:4096])
        qwd_t = [sbuf(f"qwdt{b}", [128, 512], U32) for b in range(2)]
        and_d = [sbuf(f"andd{b}", [128, 512], U32) for b in range(2)]
        wdeq_d = [sbuf(f"wdd{b}", [128, 1024], BF16) for b in range(2)]
        wdeq_d.append(qw_t[0][:].bitcast(BF16)[:, 0:1024])
        wdeq_d.append(qw_t[1][:].bitcast(BF16)[:, 0:1024])
        wdeq_d.append(sc_t[0][:].bitcast(BF16)[:, 0:1024])
        wdeq_d.append(sc_t[0][:].bitcast(BF16)[:, 1024:2048])
        wdeq_d.append(sc_t[1][:].bitcast(BF16)[:, 0:1024])
        wdeq_d.append(sc_t[1][:].bitcast(BF16)[:, 1024:2048])
        scd_e = [sbuf(f"scde{b}", [128, 512], F32) for b in range(2)]
        szqd = sbuf("szqd", [86, 512], BF16)
        g_sb = sbuf("gsb", [128, 4 * T], F32)
        osb = [sbuf(f"osb{b}", [128, 512], F32) for b in range(2)]
        e_mask = [sbuf(f"emask{b}", [128, 128], BF16) for b in range(2)]
        e_mq = sbuf("emqs", [128, 4 * 128], BF16)
        e_xg = sbuf("exgs", [128, 4 * 32], BF16)
        e_hg = sbuf("ehgs", [128, 11 * 86], BF16)
        id_b = sbuf("idb_s", [128, 128], BF16)
        id_f = sbuf("idf_s", [128, 128], F32)
        ln_sm = [sbuf(f"lnsm{b}", [128, 1], F32) for b in range(2)]
        ln_pp = [[sbuf(f"lnpp{q}_{b}", [128, 1], F32) for b in range(2)] for q in range(4)]
        ln_ss = [sbuf(f"lnss{b}", [128, 1], F32) for b in range(2)]
        ln_v2 = [sbuf(f"lnv2{b}", [128, 1], F32) for b in range(2)]
        ln_var = [sbuf(f"lnv{b}", [128, 1], F32) for b in range(2)]
        ln_std = [sbuf(f"lnst{b}", [128, 1], F32) for b in range(2)]
        ln_rstd = [sbuf(f"lnr{b}", [128, 1], F32) for b in range(2)]
        ln_nb = [sbuf(f"lnnb{b}", [128, 1], F32) for b in range(2)]
        acf = sbuf("acf", [128, 1], F32)
        sqh = sbuf("sqh", [128, 1024], F32)
        zqd_s = mask_t[0:86, 0:512].bitcast(U32)
        qzd_t = mask_t[0:86, 512:576].bitcast(U32)
        scd_n = mask_t[0:86, 576:1088].bitcast(F32)
        x2 = [mask_t[:, 1088:1600].bitcast(F32), sqh[:, 512:1024]]
        mtr = sqh[:, 0:1024].bitcast(I32)

        ps = ctx.enter_context(nc.psum_tensor("ps", [128, 4096], F32))
        p_xtr = [ps[:, 0:512].bitcast(BF16), ps[:, 512:1024].bitcast(BF16)]
        p_xg = ps[0:32, 1024:2048]
        p_gate = ps[:, 0:1024]
        p_up = ps[:, 1024:2048]
        p_hg = ps[0:86, 2048:3072]
        p_bm = ps[:, 3072:3200]
        p_bmt = ps[:, 3584:3648].bitcast(BF16)
        p_bmxf = ps[:, 3712:3840]
        p_o2 = [ps[:, hc * 1024:(hc + 1) * 1024] for hc in range(4)]
        p_wt = [ps[:, 0:512], ps[:, 512:1024], ps[:, 3072:3584], ps[:, 3584:4096]]
        p_sp = [ps[:, 1024:2048], ps[:, 2048:3072]]
        p_ot = [ps[:, 2048:2560], ps[:, 2560:3072]]

        sems = {}
        for s_ in Sched.sems:
            sems[s_] = ctx.enter_context(nc.semaphore(s_))

        S = Sched()

        def dma(eng, sem, outap, inap, waits=(), ev=None):
            def fn(e, _o=outap, _i=inap):
                return e.dma_start(out=_o, in_=_i)
            S.add(eng, fn, waits=waits, ev=ev, sem=sem, amt=16)

        def op(eng, sem, fn, waits=(), ev=None):
            S.add(eng, fn, waits=waits, ev=ev, sem=sem if ev else None)

        op("scalar", "ac", lambda s_: s_.memzero(acf[:]), waits=(), ev="acf0")

        dma("sync", "sq", e_mq[:], emq.rearrange("c p m -> p c m"), ev="c_emq")
        dma("sync", "sq", e_xg[:], exg.rearrange("c p m -> p c m"), ev="c_exg")
        dma("sync", "sq", e_hg[:], ehg.rearrange("c p m -> p c m"), ev="c_ehg")
        dma("sync", "sq", id_b[:], idb, ev="c_idb")
        dma("sync", "sq", id_f[:], idf, ev="c_idf")

        def p0_tr(tc):
            for hb in range(4):
                par = hb % 2
                w = [f"lnL{tc}", "c_idb"]
                if hb >= 2:
                    w.append(f"p0ev_{tc}_{hb - 2}")
                elif tc >= 1:
                    w.append(f"p0ev_{tc - 1}_{hb + 2}")

                def tr(t, tc=tc, hb=hb, par=par):
                    r = None
                    for i in range(8):
                        c = hb * 8 + i
                        r = t.transpose(p_xtr[par][:, i * 128:(i + 1) * 128],
                                        v_xntok[tc % 2][:, c * 128:(c + 1) * 128], id_b[:])
                    return r
                op("tensor", "pe", tr, waits=w,
                   ev=f"p0tr_{tc}" if hb == 3 else f"p0trh_{tc}_{hb}")

                def tev(v, tc=tc, hb=hb, par=par):
                    dst = xn[:, :].rearrange("p (c t) -> p c t", c=32)[
                        :, hb * 8:(hb + 1) * 8, tc * 128:(tc + 1) * 128]
                    return v.tensor_copy(dst, p_xtr[par][:].rearrange("p (c t) -> p c t", c=8))
                op("vector", "ve", tev,
                   waits=[f"p0tr_{tc}" if hb == 3 else f"p0trh_{tc}_{hb}"],
                   ev=f"p0ev_{tc}_{hb}")

        for tc in range(8):
            b = tc % 2
            w_ld = [f"lnL{tc - 2}"] if tc >= 2 else []
            dma("sync", "sq", v_xt[b][:], xp[tc * 128:(tc + 1) * 128, :], waits=w_ld,
                ev=f"xld{tc}")

            def lnA(v, b=b):
                return v.reduce_sum(ln_sm[b][:], v_xt[b][:], axis=AX)
            op("vector", "ve", lnA, waits=[f"xld{tc}"], ev=f"ln0{tc}")

            prev = f"ln0{tc}"
            for qq_ in range(4):
                def lnSq(v, b=b, qq_=qq_):
                    return v.tensor_tensor(sqh[:], v_xt[b][:, qq_ * 1024:(qq_ + 1) * 1024],
                                           v_xt[b][:, qq_ * 1024:(qq_ + 1) * 1024], OP.mult)
                op("vector", "ve", lnSq, waits=[prev], ev=f"lnq{tc}_{qq_}")

                def lnRd(v, b=b, qq_=qq_):
                    return v.reduce_sum(ln_pp[qq_][b][:], sqh[:], axis=AX)
                op("vector", "ve", lnRd, waits=[f"lnq{tc}_{qq_}"], ev=f"lnr{tc}_{qq_}")
                prev = f"lnr{tc}_{qq_}"

            def lnE(v, b=b):
                v.tensor_tensor(ln_ss[b][:], ln_pp[0][b][:], ln_pp[1][b][:], OP.add)
                v.tensor_tensor(ln_v2[b][:], ln_pp[2][b][:], ln_pp[3][b][:], OP.add)
                return v.tensor_scalar(ln_sm[b][:], ln_sm[b][:], 1.0 / D, None, OP.mult)
            op("vector", "ve", lnE, waits=[prev], ev=f"lnE{tc}")

            def lnF(v, b=b):
                v.tensor_tensor(ln_var[b][:], ln_ss[b][:], ln_v2[b][:], OP.add)
                return v.tensor_tensor(ln_std[b][:], ln_sm[b][:], ln_sm[b][:], OP.mult)
            op("vector", "ve", lnF, waits=[f"lnE{tc}"], ev=f"lnF{tc}")

            def lnG(v, b=b):
                return v.tensor_scalar(ln_var[b][:], ln_var[b][:], 1.0 / D, None, OP.mult)
            op("vector", "ve", lnG, waits=[f"lnF{tc}"], ev=f"lnG{tc}")

            def lnH0(v, b=b):
                return v.tensor_tensor(ln_var[b][:], ln_var[b][:], ln_std[b][:], OP.subtract)
            op("vector", "ve", lnH0, waits=[f"lnG{tc}"], ev=f"lnH0{tc}")

            def lnH(v, b=b):
                return v.tensor_scalar(ln_var[b][:], ln_var[b][:], 1e-5, None, OP.add)
            op("vector", "ve", lnH, waits=[f"lnH0{tc}"], ev=f"lnH{tc}")

            def sqrt_f(s_, b=b):
                s_.activation(ln_std[b][:], ln_var[b][:], AF.Sqrt)
                return s_.activation(acf[:], acf[:], AF.Copy)
            op("scalar", "ac", sqrt_f, waits=[f"lnH{tc}", "NOW:ac"], ev=f"lnI{tc}")

            op("vector", "ve", lambda v, b=b: v.reciprocal(ln_rstd[b][:], ln_std[b][:]),
               waits=[f"lnI{tc}"], ev=f"lnJ{tc}")

            def lnK(v, b=b):
                return v.tensor_scalar(ln_nb[b][:], ln_sm[b][:], ln_rstd[b][:], -1.0,
                                       OP.mult, OP.mult)
            op("vector", "ve", lnK, waits=[f"lnJ{tc}"], ev=f"lnK{tc}")

            wn = [f"lnK{tc}"]
            if tc >= 2:
                wn.append(f"p0tr_{tc - 2}")
            def lnL(v, b=b):
                return v.tensor_scalar(v_xntok[b][:], v_xt[b][:], ln_rstd[b][:], ln_nb[b][:],
                                       OP.mult, OP.add)
            op("vector", "ve", lnL, waits=wn, ev=f"lnL{tc}")

            if tc >= 1:
                p0_tr(tc - 1)
        p0_tr(7)

        for c in range(32):
            q = c // 8
            w = ["p0ev_7_3", "c_exg"] if c == 0 else []

            def xgmm(t, c=c, q=q):
                r = None
                for th in range(2):
                    r = t.matmul(p_xg[:, th * 512:(th + 1) * 512],
                                 e_xg[:, q * 32:(q + 1) * 32],
                                 xn[:, c * T + th * 512: c * T + (th + 1) * 512],
                                 start=(c == 0), stop=(c == 31), skip_group_check=True)
                return r
            op("tensor", "pe", xgmm, waits=w,
               ev="xgmm" if c == 31 else None)
        op("vector", "ve", lambda v: v.tensor_scalar(xgtn[:], p_xg[:], -1.0, None, OP.mult),
           waits=["xgmm"], ev="xgtn")
        if debug:
            dma("gpsimd", "gq", dbg["xg"], xgtn[:], waits=["xgtn"], ev="dbg_xg")
            dma("gpsimd", "gq", dbg["xn"], xn[:], waits=["p0ev_7_3"], ev="dbg_xn")

        def p1_loads(pc):
            qp, jp, rp0, rpw, c0 = PCS[pc]
            W = 2 * rpw
            b = pc % 2
            wl = [f"deq{pc - 2}"] if pc >= 2 else []
            dma("sync", "sq",
                qw_t[b][:].rearrange("p (q w) -> p q w", q=4)[:, :, 0:W],
                qwp[:, c0:c0 + W].rearrange("(q p) w -> p q w", q=4),
                waits=wl, ev=f"qwld{pc}")
            for q in range(4):
                dma("sync", "sq",
                    sc_t[b][:, q * 256:q * 256 + W],
                    scpe[q][:, c0:c0 + W],
                    ev=f"scld{pc}_{q}")
            dma("sync", "sq", sc_c[b][:, 0:W], scp[:, c0:c0 + W], ev=f"sccld{pc}")
            dma("sync", "sq", qz_t[b][:, 0:rpw], qzg[:, rp0:rp0 + rpw], ev=f"qzg{pc}")
            dma("sync", "sq", qz_t[b][:, 128:128 + rpw], qzg[:, RPD + rp0:RPD + rp0 + rpw],
                ev=f"qzu{pc}")

        def p1_deq(pc):
            qp, jp, rp0, rpw, c0 = PCS[pc]
            W = 2 * rpw
            b = pc % 2
            wd = [f"qwld{pc}", f"sccld{pc}", f"qzu{pc}", f"qzg{pc}"] \
                + [f"scld{pc}_{q}" for q in range(4)]
            if pc >= 2:
                wd.append(f"umm{pc - 2}")
            if pc == 0:
                wd += ["p0ev_7_3", "xgtn", "p0tr_7", "xgmm"]

            def deq(v, b=b, rpw=rpw, W=W, jp=jp):
                r = None
                for h in range(2):
                    for q in (2 * h, 2 * h + 1):
                        for s in range(4):
                            ti = (q % 2) * 4 + s
                            v.tensor_scalar(v_and[:, ti * 256:ti * 256 + W],
                                            qw_t[b][:, q * 256:q * 256 + W],
                                            ANDMASKS[s], None, OP.bitwise_and)
                    if h == 0:
                        v.tensor_scalar(zq_s[:, 0:rpw], qz_t[b][:, 0:rpw], 4 * jp, 15,
                                        OP.logical_shift_right, OP.bitwise_and)
                        v.tensor_scalar(zq_s[:, 128:128 + rpw], qz_t[b][:, 128:128 + rpw],
                                        4 * jp, 15, OP.logical_shift_right, OP.bitwise_and)
                    else:
                        v.scalar_tensor_tensor(szqm[b][:, 0:rpw], zq_s[:, 0:rpw], 1.0,
                                               sc_c[b][:, 0:rpw], OP.add, OP.mult)
                        v.scalar_tensor_tensor(szqm[b][:, 128:128 + rpw],
                                               zq_s[:, 128:128 + rpw], 1.0,
                                               sc_c[b][:, rpw:W], OP.add, OP.mult)
                    for q in (2 * h, 2 * h + 1):
                        for s in range(4):
                            ti = (q % 2) * 4 + s
                            t0 = q * 4 + s
                            r = v.scalar_tensor_tensor(
                                v_wdeq[b][:, t0 * 512:t0 * 512 + 2 * W].rearrange(
                                    "p (w two) -> p w two", two=2),
                                v_and[:, ti * 256:ti * 256 + W].bitcast(U16).rearrange(
                                    "p (w two) -> p w two", two=2),
                                2.0 ** (-4 * s),
                                sc_t[b][:, q * 256:q * 256 + W].unsqueeze(2).broadcast_to(
                                    [128, W, 2]),
                                OP.mult, OP.mult)
                return r
            op("vector", "ve", deq, waits=wd, ev=f"deq{pc}")

        p1_loads(0)
        p1_loads(1)
        p1_deq(0)

        for pc in range(NPC):
            qp, jp, rp0, rpw, c0 = PCS[pc]
            W = 2 * rpw
            b = pc % 2

            if pc < 64:
                mc, half = pc // 2, pc % 2
                wm = []
                if pc >= 1:
                    wm.append(f"mtree{pc - 1}")
                if pc >= 4:
                    wm.append(f"mcvt{2 * (mc - 2)}")
                dma("scalar", "aq", mask_t[:],
                    msk[mc * 128:(mc + 1) * 128, half * 2048:(half + 1) * 2048],
                    waits=wm, ev=f"mld{pc}")

                def mtree(g, mc=mc, half=half):
                    src = mask_t[:].rearrange("p (k c) -> p k c", k=64)
                    o1 = mtr.rearrange("p (k c) -> p k c", k=64)
                    o2 = mtr[:, 0:512].rearrange("p (k c) -> p k c", k=64)
                    o3 = mtr[:, 0:256].rearrange("p (k c) -> p k c", k=64)
                    o4 = mtr[:, 0:128].rearrange("p (k c) -> p k c", k=64)
                    g.tensor_tensor(o1, src[:, :, 0:16], src[:, :, 16:32], OP.add)
                    g.tensor_tensor(o2, o1[:, :, 0:8], o1[:, :, 8:16], OP.add)
                    g.tensor_tensor(o3, o2[:, :, 0:4], o2[:, :, 4:8], OP.add)
                    g.tensor_tensor(o4, o3[:, :, 0:2], o3[:, :, 2:4], OP.add)
                    dst = colsum[mc % 2][:, half * 64:(half + 1) * 64]
                    return g.tensor_tensor(dst.unsqueeze(2), o4[:, :, 0:1], o4[:, :, 1:2], OP.add)
                wtr_ = [f"mld{pc}"] + (["lnr7_3"] if pc < 2 else [])
                op("vector", "ve", mtree, waits=wtr_, ev=f"mtree{pc}")

                if half == 1:
                    wcv_ = [f"mtree{pc}"]
                    if mc >= 4:
                        wcv_.append(f"bmm{mc - 4}")
                    op("vector", "ve",
                       lambda v, mc=mc: v.tensor_copy(colsb[mc % 4][:], colsum[mc % 2][:]),
                       waits=wcv_, ev=f"mcvt{2 * mc}")

            if pc >= 6 and pc % 2 == 0 and (pc - 6) // 2 < 32:
                mcb = (pc - 6) // 2
                wem = [f"bmm{mcb - 2}"] if mcb >= 2 else []
                dma("scalar", "aq", e_mask[mcb % 2][:], em[mcb], waits=wem, ev=f"emld{mcb}")

                def bmm(t, mcb=mcb):
                    return t.matmul(p_bm[:, 0:128], e_mask[mcb % 2][:],
                                    colsb[mcb % 4][:], start=(mcb == 0), stop=(mcb == 31),
                                    skip_group_check=True)
                op("tensor", "pe", bmm, waits=[f"mcvt{2 * mcb}", f"emld{mcb}"], ev=f"bmm{mcb}")

            wg = [f"deq{pc}"]
            if pc >= 1:
                wg.append(f"silu{pc - 1}")

            def gmm(t, b=b, rpw=rpw, W=W):
                r = None
                for q in range(4):
                    for s in range(4):
                        for sel in range(2):
                            c = q * 8 + s + 4 * sel
                            t0 = q * 4 + s
                            lhs = v_wdeq[b][:, t0 * 512:t0 * 512 + 2 * W].rearrange(
                                "p (w two) -> p w two", two=2)[:, 0:rpw, sel]
                            for th in range(2):
                                t.matmul(p_gate[0:rpw, th * 512:(th + 1) * 512], lhs,
                                         xn[:, c * T + th * 512: c * T + (th + 1) * 512],
                                         start=(q == 0 and s == 0 and sel == 0),
                                         stop=False, skip_group_check=True)
                for th in range(2):
                    r = t.matmul(p_gate[0:rpw, th * 512:(th + 1) * 512],
                                 szqm[b][0:32, 0:rpw],
                                 xgtn[0:32, th * 512:(th + 1) * 512],
                                 start=False, stop=True, skip_group_check=True)
                return r
            op("tensor", "pe", gmm, waits=wg, ev=f"gmm{pc}")

            if pc >= 1:
                qpm, jpm, rp0m, rpwm, _ = PCS[pc - 1]

                def hgmm(t, pcm=pc - 1, qpm=qpm, rpwm=rpwm):
                    r = None
                    for th in range(2):
                        r = t.matmul(p_hg[:, th * 512:(th + 1) * 512],
                                     e_hg[0:rpwm, qpm * 86:(qpm + 1) * 86],
                                     a_hact[pcm % 2][0:rpwm, th * 512:(th + 1) * 512],
                                     start=(pcm == 0), stop=False, skip_group_check=True)
                    return r
                op("tensor", "pe", hgmm, waits=[f"hact{pc - 1}", "c_ehg"],
                   ev=f"hgmm{pc - 1}")

            wu = []
            if pc >= 1:
                wu.append(f"hact{pc - 1}")

            def umm(t, b=b, rpw=rpw, W=W):
                r = None
                for q in range(4):
                    for s in range(4):
                        for sel in range(2):
                            c = q * 8 + s + 4 * sel
                            t0 = q * 4 + s
                            lhs = v_wdeq[b][:, t0 * 512:t0 * 512 + 2 * W].rearrange(
                                "p (w two) -> p w two", two=2)[:, rpw:W, sel]
                            for th in range(2):
                                t.matmul(p_up[0:rpw, th * 512:(th + 1) * 512], lhs,
                                         xn[:, c * T + th * 512: c * T + (th + 1) * 512],
                                         start=(q == 0 and s == 0 and sel == 0),
                                         stop=False, skip_group_check=True)
                for th in range(2):
                    r = t.matmul(p_up[0:rpw, th * 512:(th + 1) * 512],
                                 szqm[b][0:32, 128:128 + rpw],
                                 xgtn[0:32, th * 512:(th + 1) * 512],
                                 start=False, stop=True, skip_group_check=True)
                return r
            op("tensor", "pe", umm, waits=wu, ev=f"umm{pc}")

            ws = [f"gmm{pc}"]
            if pc >= 1:
                ws.append(f"hact{pc - 1}")
            def silu_f(s_, rpw=rpw):
                s_.activation(a_silu[0:rpw, :], p_gate[0:rpw, :], AF.Silu)
                return s_.activation(acf[:], acf[:], AF.Copy)
            op("scalar", "ac", silu_f, waits=ws + ["NOW:ac"], ev=f"silu{pc}")

            if pc + 1 < NPC:
                p1_deq(pc + 1)

            wh = [f"umm{pc}", f"silu{pc}"]
            if pc >= 2:
                wh += [f"hsc{pc - 2}", f"hgmm{pc - 2}"]

            def hev(v, pc=pc, rpw=rpw):
                return v.tensor_tensor(a_hact[pc % 2][0:rpw, :], a_silu[0:rpw, :],
                                       p_up[0:rpw, :], OP.mult)
            op("vector", "ve", hev, waits=wh, ev=f"hact{pc}")

            if pc + 2 < NPC:
                p1_loads(pc + 2)

            pos = JPOS[jp]
            dma("gpsimd", "gq", hsc[qp][:, pos * T:(pos + 1) * T], a_hact[pc % 2][:],
                waits=[f"hact{pc}"], ev=f"hsc{pc}")
            if debug and pc == 0:
                dma("gpsimd", "gq", dbg["h0"], a_hact[0][:], waits=[f"hact{pc}"], ev="dbg_h0")

        qpm, jpm, rp0m, rpwm, _ = PCS[NPC - 1]

        def hgl(t, qpm=qpm, rpwm=rpwm):
            r = None
            for th in range(2):
                r = t.matmul(p_hg[:, th * 512:(th + 1) * 512],
                             e_hg[0:rpwm, qpm * 86:(qpm + 1) * 86],
                             a_hact[(NPC - 1) % 2][0:rpwm, th * 512:(th + 1) * 512],
                             start=False, stop=True, skip_group_check=True)
            return r
        op("tensor", "pe", hgl, waits=[f"hact{NPC - 1}"], ev=f"hgmm{NPC - 1}")
        op("vector", "ve", lambda v: v.tensor_scalar(hgn[:], p_hg[:], -1.0, None, OP.mult),
           waits=[f"hgmm{NPC - 1}"], ev="hgn")

        op("vector", "ve", lambda v: v.tensor_scalar(bm_sb[:], p_bm[:, 0:128], 0.0, None, OP.is_gt),
           waits=["bmm31"], ev="bmsb")
        op("tensor", "pe", lambda t: t.transpose(p_bmt[:, 0:128], bm_sb[:], id_b[:]),
           waits=["bmsb", "c_idb"], ev="bmtr")
        op("vector", "ve", lambda v: v.tensor_copy(bmt_sb[:], p_bmt[:, 0:128]),
           waits=["bmtr"], ev="bmtev")
        for q in range(4):
            def bx(t, q=q):
                return t.matmul(p_bmxf[:, 0:128], e_mq[:, q * 128:(q + 1) * 128],
                                bmt_sb[:], start=True, stop=True, skip_group_check=True)
            wq_ = (["bmtev"] if q == 0 else [f"bmxe{q - 1}"]) + ["c_emq"]
            op("tensor", "pe", bx, waits=wq_, ev=f"bmx{q}")
            op("vector", "ve",
               lambda v, q=q: v.tensor_copy(bmx[:, q * 128:(q + 1) * 128], p_bmxf[:, 0:128]),
               waits=[f"bmx{q}"], ev=f"bmxe{q}")
        if debug:
            dma("gpsimd", "gq", dbg["bm"], bm_sb[:], waits=["bmsb"], ev="dbg_bm")
            dma("gpsimd", "gq", dbg["hg"], hgn[:], waits=["hgn"], ev="dbg_hg")

        for sp in range(NSUP):
            H0 = 512 * sp
            wzl = [f"szqd{sp - 1}", f"o2{sp - 1}"] if sp >= 1 else ["mtree63", "bmm31"]
            dma("sync", "sq", qzd_t, qzd[:, 64 * sp:64 * (sp + 1)], waits=wzl,
                ev=f"qzdld{sp}")
            dma("sync", "sq", scd_n, scd[:, H0:H0 + 512], ev=f"scdnld{sp}")

            def zunA(v):
                r = None
                for j in range(8):
                    r = v.tensor_scalar(
                        zqd_s.rearrange("p (w e) -> p w e", e=8)[:, :, j].unsqueeze(2),
                        qzd_t.unsqueeze(2), 4 * j, 15,
                        OP.logical_shift_right, OP.bitwise_and)
                return r
            op("vector", "ve", zunA, waits=[f"qzdld{sp}", f"scdnld{sp}"], ev=f"zunA{sp}")

            def zunB(v):
                return v.scalar_tensor_tensor(szqd[:], zqd_s, 1.0, scd_n, OP.add, OP.mult)
            op("vector", "ve", zunB, waits=[f"zunA{sp}"], ev=f"szqd{sp}")
            if debug and sp == 1:
                dma("gpsimd", "gq", dbg["szq"], szqd[:], waits=["szqd1"], ev="dbg_szq")

            def dn_deq(qi):
                rp0 = 128 * qi
                rpw = min(128, RPD - rp0)
                b3 = qi % 2
                wq_ = []
                if qi >= 2:
                    wq_.append(f"ddB{sp}_{qi - 2}_3")
                elif sp >= 1:
                    wq_.append(f"ddB{sp - 1}_{10 - qi}_3")
                dma("scalar", "aq", qwd_t[b3][0:rpw, :], qwd[rp0:rp0 + rpw, H0:H0 + 512],
                    waits=wq_, ev=f"qwdld{sp}_{qi}")
                dma("scalar", "aq", scd_e[b3][:],
                    scde[qi][:, H0:H0 + 512],
                    ev=f"scdld{sp}_{qi}")
                for s4 in range(4):
                    bs = (qi % 2) * 4 + s4
                    wv = [f"qwdld{sp}_{qi}", f"scdld{sp}_{qi}"]
                    if sp == 0 and qi == 0 and s4 == 0:
                        wv.append("bmxe3")
                    if qi >= 2:
                        wv.append(f"dnm{sp}_{qi - 2}_{s4}")
                    elif sp >= 1:
                        wv.append(f"dnm{sp - 1}_{9 + qi}_{s4}")

                    def ddA(v, s=s4, b3=b3):
                        return v.tensor_scalar(and_d[s % 2][:], qwd_t[b3][:], ANDMASKS[s], None,
                                               OP.bitwise_and)
                    op("vector", "ve", ddA, waits=wv, ev=f"ddA{sp}_{qi}_{s4}")

                    def ddB(v, s=s4, bs=bs, b3=b3):
                        wd_ = wdeq_d[bs][:] if bs < 2 else wdeq_d[bs]
                        return v.scalar_tensor_tensor(
                            wd_.rearrange("p (w two) -> p w two", two=2),
                            and_d[s % 2][:].bitcast(U16).rearrange("p (w two) -> p w two", two=2),
                            2.0 ** (-4 * s),
                            scd_e[b3][:].unsqueeze(2).broadcast_to([128, 512, 2]),
                            OP.mult, OP.mult)
                    op("vector", "ve", ddB, waits=[f"ddA{sp}_{qi}_{s4}"], ev=f"ddB{sp}_{qi}_{s4}")

            first_dn = True
            if sp == 0:
                dn_deq(0)
                dn_deq(1)
            for qi in range(11):
                rp0 = 128 * qi
                rpw = min(128, RPD - rp0)
                for s in range(4):
                    bs = (qi % 2) * 4 + s
                    hb_ = (4 * qi + s) % 4
                    k_ = 4 * qi + s
                    wh_ = []
                    if k_ >= 4:
                        pk = k_ - 4
                        wh_.append(f"dnm{sp}_{pk // 4}_{pk % 4}")
                    elif sp >= 1:
                        pk = 40 + k_
                        wh_.append(f"dnm{sp - 1}_{pk // 4}_{pk % 4}")
                    elif s >= 2:
                        wh_.append(f"umm{NPC - 1}")
                    q_, qs_ = (("gpsimd", "gq") if k_ % 2 == 0 else ("scalar", "aq"))
                    dma(q_, qs_, hin[hb_][:],
                        hsc[qi][:, 2 * s * T:(2 * s + 2) * T],
                        waits=wh_, ev=f"hinld{sp}_{qi}_{s}")

                    wmm = [f"ddB{sp}_{qi}_{s}", f"hinld{sp}_{qi}_{s}"]
                    if first_dn:
                        if sp >= 1:
                            wmm += [f"gsb{sp - 1}", f"oev{sp - 1}_7"]
                        else:
                            wmm += ["hgn", "bmxe3"]

                    def dmm(t, s=s, bs=bs, hb_=hb_, rpw=rpw, first=first_dn):
                        r = None
                        for jj in range(2):
                            for hc in range(4):
                                wd_ = wdeq_d[bs][:] if bs < 2 else wdeq_d[bs]
                                lhs = wd_.rearrange("p (w two) -> p w two", two=2)[
                                    0:rpw, hc * 128:(hc + 1) * 128, jj]
                                for th in range(2):
                                    r = t.matmul(
                                        p_o2[hc][0:128, th * 512:(th + 1) * 512], lhs,
                                        hin[hb_][0:rpw, jj * T + th * 512: jj * T + (th + 1) * 512],
                                        start=(first and jj == 0),
                                        stop=False, skip_group_check=True)
                        return r
                    op("tensor", "pe", dmm, waits=wmm, ev=f"dnm{sp}_{qi}_{s}")
                    first_dn = False
                if qi + 2 < 11:
                    dn_deq(qi + 2)

            def dz(t):
                r = None
                for hc in range(4):
                    for th in range(2):
                        r = t.matmul(p_o2[hc][0:128, th * 512:(th + 1) * 512],
                                     szqd[0:86, hc * 128:(hc + 1) * 128],
                                     hgn[0:86, th * 512:(th + 1) * 512],
                                     start=False, stop=True, skip_group_check=True)
                return r
            wdz = [f"szqd{sp}", f"dnm{sp}_10_3"]
            if debug and sp == 1:
                wdz.append("dbg_szq")
            op("tensor", "pe", dz, waits=wdz, ev=f"o2{sp}")

            def gsl(s_):
                for hc in range(4):
                    s_.activation(g_sb[:, hc * T:(hc + 1) * T], p_o2[hc][:], AF.Silu)
                return s_.activation(acf[:], acf[:], AF.Copy)
            wgs = [f"o2{sp}"]
            if debug and sp == 1:
                wgs.append("dbg_g0")
            if debug and sp == 2:
                wgs.append("dbg_g1")
            op("scalar", "ac", gsl, waits=wgs + ["NOW:ac"], ev=f"gsb{sp}")
            if debug and sp == 0:
                dma("gpsimd", "gq", dbg["g0"], g_sb[:, 0:T], waits=["gsb0"], ev="dbg_g0")
            if debug and sp == 1:
                dma("gpsimd", "gq", dbg["g1"], g_sb[:, 0:T], waits=["gsb1"], ev="dbg_g1")

            def attn_prep(ncl):
                gc = 4 * sp + ncl
                for h2 in range(4):
                    ww = []
                    if sp == 0 and ncl == 0:
                        ww = [f"umm{NPC - 1}"]
                    if h2 >= 2:
                        ww.append(f"wcv{(sp, ncl, h2 - 2)}")
                    elif ncl >= 1:
                        ww.append(f"wcv{(sp, ncl - 1, h2 + 2)}")
                    elif sp >= 1:
                        ww.append(f"wcv{(sp - 1, 3, h2 + 2)}")
                    dma("sync", "sq", v_wrow[:, (h2 % 2) * 1024:(h2 % 2 + 1) * 1024],
                        wp[gc * 128:(gc + 1) * 128, h2 * 1024:(h2 + 1) * 1024],
                        waits=ww, ev=f"wld{(sp, ncl, h2)}")
                    wc = [f"wld{(sp, ncl, h2)}"]
                    if ncl >= 1:
                        wc.append(f"str{(sp, ncl - 1, 2 * h2 + 1)}")
                    elif sp >= 1:
                        wc.append(f"str{(sp - 1, 3, 2 * h2 + 1)}")

                    def wcv_f(s_, h2=h2):
                        s_.activation(v_wb[:, h2 * 1024:(h2 + 1) * 1024],
                                      v_wrow[:, (h2 % 2) * 1024:(h2 % 2 + 1) * 1024], AF.Copy)
                        return s_.activation(acf[:], acf[:], AF.Copy)
                    op("scalar", "ac", wcv_f, waits=wc + ["NOW:ac"], ev=f"wcv{(sp, ncl, h2)}")

                for bt in range(8):
                    par = bt % 4
                    wt_ = [f"wcv{(sp, ncl, bt // 2)}"]
                    if bt < 4 and ncl == 0:
                        wt_.append(f"gsb{sp}")
                    if bt >= 4:
                        wt_.append(f"wte{(sp, ncl, bt - 4)}")
                    elif ncl >= 1:
                        wt_.append(f"wte{(sp, ncl - 1, bt + 4)}")
                    elif sp >= 1:
                        wt_.append(f"wte{(sp - 1, 3, bt + 4)}")

                    def wtr(t, bt=bt, par=par):
                        r = None
                        for i in range(4):
                            c = bt * 4 + i
                            r = t.matmul(p_wt[par][:, i * 128:(i + 1) * 128],
                                         v_wb[:, c * 128:(c + 1) * 128], id_b[:],
                                         start=True, stop=True, skip_group_check=True)
                        return r
                    op("tensor", "pe", wtr, waits=wt_, ev=f"str{(sp, ncl, bt)}")

                    def wte(v, bt=bt, par=par, gc=gc):
                        qb = bt // 2
                        m_ = bmx[:, qb * 128 + 4 * gc: qb * 128 + 4 * gc + 4]
                        return v.tensor_tensor(
                            v_wtb[:, bt * 512:(bt + 1) * 512].rearrange(
                                "p (k a e) -> p k a e", k=4, a=4),
                            p_wt[par][:].rearrange("p (k a e) -> p k a e", k=4, a=4),
                            m_.unsqueeze(1).unsqueeze(3).broadcast_to([128, 4, 4, 32]),
                            OP.mult)
                    op("vector", "ve", wte, waits=[f"str{(sp, ncl, bt)}"],
                       ev=f"wte{(sp, ncl, bt)}")

            def attn_cmb(ncl):
                def cmb(v, ncl=ncl):
                    return v.tensor_tensor(v_comb[:, ncl * T:(ncl + 1) * T],
                                           p_sp[ncl % 2][:, 0:1024],
                                           g_sb[:, ncl * T:(ncl + 1) * T], OP.mult)
                wcmb = [f"smm{sp}_{ncl}", f"gsb{sp}"]
                if debug and sp == 2 and ncl == 0:
                    wcmb.append("dbg_c1")
                op("vector", "ve", cmb, waits=wcmb, ev=f"comb{sp}_{ncl}")
                if debug and sp == 1 and ncl == 0:
                    dma("gpsimd", "gq", dbg["c1"], v_comb[:, 0:T], waits=["comb1_0"], ev="dbg_c1")

            def attn_smm(ncl):
                wa = [f"wte{(sp, ncl, 7)}"]
                if ncl >= 2:
                    wa.append(f"comb{sp}_{ncl - 2}")
                elif sp >= 1:
                    wa.append(f"comb{sp - 1}_{ncl + 2}")
                    if ncl == 1:
                        wa.append(f"oev{sp - 1}_7")
                else:
                    wa.append(f"gsb{sp}")
                    if ncl == 0:
                        wa.append(f"hsc{NPC - 1}")

                def smm(t, ncl=ncl):
                    r = None
                    for c in range(32):
                        for th in range(2):
                            r = t.matmul(p_sp[ncl % 2][:, th * 512:(th + 1) * 512],
                                         v_wtb[:, c * 128:(c + 1) * 128],
                                         xn[:, c * T + th * 512: c * T + (th + 1) * 512],
                                         start=(c == 0), stop=(c == 31),
                                         skip_group_check=True)
                    return r
                op("tensor", "pe", smm, waits=wa, ev=f"smm{sp}_{ncl}")

            if sp + 1 < NSUP:
                sp_s, H0_s = sp, H0
                sp, H0 = sp + 1, 512 * (sp + 1)
                dn_deq(0)
                dn_deq(1)
                sp, H0 = sp_s, H0_s
            for ncl in range(4):
                attn_prep(ncl)
                if ncl >= 1:
                    attn_cmb(ncl - 1)
                attn_smm(ncl)
            attn_cmb(3)

            for tc in range(8):
                pto = tc % 2
                b4 = tc % 2
                wo = [f"comb{sp}_3", f"comb{sp}_2", f"comb{sp}_1", f"comb{sp}_0", "c_idf"]
                if tc >= 2:
                    wo.append(f"oev{sp}_{tc - 2}")

                def otr(t, tc=tc, pto=pto):
                    r = None
                    for ncl in range(4):
                        r = t.transpose(p_ot[pto][:, ncl * 128:(ncl + 1) * 128],
                                        v_comb[:, ncl * T + tc * 128: ncl * T + (tc + 1) * 128],
                                        id_f[:])
                    return r
                op("tensor", "pe", otr, waits=wo, ev=f"otr{sp}_{tc}")

                wx = [f"oev{sp}_{tc - 2}"] if tc >= 2 else \
                    ([f"ost{sp - 1}_{tc + 6}"] if sp >= 1 else ["mtree63"])
                dma("sync", "sq", x2[b4], xr[tc * 128:(tc + 1) * 128, H0:H0 + 512],
                    waits=wx, ev=f"xr{sp}_{tc}")

                wv_ = [f"otr{sp}_{tc}", f"xr{sp}_{tc}"]
                if tc >= 2:
                    wv_.append(f"ost{sp}_{tc - 2}")
                elif sp >= 1:
                    wv_.append(f"ost{sp - 1}_{tc + 6}")

                def oev(v, pto=pto, b4=b4):
                    return v.tensor_tensor(osb[b4][:], p_ot[pto][:], x2[b4], OP.add)
                op("vector", "ve", oev, waits=wv_, ev=f"oev{sp}_{tc}")

                dma("gpsimd", "gq", out[tc * 128:(tc + 1) * 128, H0:H0 + 512], osb[b4][:],
                    waits=[f"oev{sp}_{tc}"], ev=f"ost{sp}_{tc}")

        S.semh = sems
        S.verify_no_deadlock()

        with nc.Block() as block:
            def emit(eng_name):
                def body(eng):
                    waited = {}
                    for fn, waits, ev, sem, amt in S.ops[eng_name]:
                        for w in waits:
                            s_, v_ = S.events[w]
                            if waited.get(s_, -1) < v_:
                                eng.wait_ge(sems[s_], v_)
                                waited[s_] = v_
                        instr = fn(eng)
                        if ev is not None:
                            assert instr is not None, f"op for ev {ev} returned no instr"
                            instr.then_inc(sems[sem], amt)
                    if eng_name == "gpsimd":
                        eng.wait_ge(sems["gq"], S.count["gq"])
                return body
            block.sync(emit("sync"))
            block.vector(emit("vector"))
            block.scalar(emit("scalar"))
            block.tensor(emit("tensor"))
            block.gpsimd(emit("gpsimd"))

    return nc


def _consts():
    em = np.zeros((32, 128, 128), dtype=BF)
    for c in range(32):
        for p in range(128):
            em[c, p, 4 * c + p // 32] = 1
    emq = np.zeros((4, 128, 128), dtype=BF)
    for q in range(4):
        for p in range(128):
            emq[q, 32 * q + p // 4, p] = 1
    exg = np.zeros((4, 128, 32), dtype=BF)
    for q in range(4):
        for p in range(128):
            exg[q, p, 8 * q + p // 16] = 1
    ehg = np.zeros((11, 128, 86), dtype=BF)
    for qp in range(11):
        rpw = min(128, RPD - 128 * qp)
        for p in range(rpw):
            ehg[qp, p, 8 * qp + p // 16] = 1
    idb = np.eye(128, dtype=BF)
    idf = np.eye(128, dtype=np.float32)
    return {"em": em, "emq": emq, "exg": exg, "ehg": ehg, "idb": idb, "idf": idf}


_NC_CACHE = {}


def kernel(L0_attn_data, L0_attn_weight, L0_attn_mask,
           L0_ffn_qweight_gu, L0_ffn_qzeros_gu, L0_ffn_scales_gu, L0_ffn_g_idx_gu,
           L0_ffn_qweight_down, L0_ffn_qzeros_down, L0_ffn_scales_down, L0_ffn_g_idx_down,
           _debug=False, _core_ids=None):
    x = np.ascontiguousarray(np.asarray(L0_attn_data, dtype=np.float32).reshape(8192, D))
    w = np.asarray(L0_attn_weight, dtype=np.float32)
    mask = np.asarray(L0_attn_mask, dtype=np.int32)
    qw_gu = np.asarray(L0_ffn_qweight_gu, dtype=np.uint32)
    qz_gu = np.asarray(L0_ffn_qzeros_gu, dtype=np.uint32)
    sc_gu = np.asarray(L0_ffn_scales_gu, dtype=np.float32)
    qw_d = np.asarray(L0_ffn_qweight_down, dtype=np.uint32)
    qz_d = np.asarray(L0_ffn_qzeros_down, dtype=np.uint32)
    sc_d = np.asarray(L0_ffn_scales_down, dtype=np.float32)

    wperm = np.ascontiguousarray(w[:, PI])
    qwperm = np.ascontiguousarray(qw_gu[:, COLPERM])
    scperm = np.ascontiguousarray(sc_gu[:, COLPERM])
    scpe = np.ascontiguousarray(
        np.stack([np.repeat(scperm[8 * q:8 * q + 8], 16, axis=0) for q in range(4)]))
    scde = np.zeros((11, 128, D), dtype=np.float32)
    for qi_ in range(11):
        ng = min(8, 86 - 8 * qi_)
        scde[qi_, 0:16 * ng] = np.repeat(sc_d[8 * qi_:8 * qi_ + ng], 16, axis=0)
    scde = np.ascontiguousarray(scde)
    consts = _consts()

    import os
    p2s = int(os.environ.get("P2_SERIAL", "0"))
    key = ("prog", bool(_debug), p2s)
    if key not in _NC_CACHE:
        _NC_CACHE[key] = build_program(debug=_debug, p2_serial=p2s)
    nc = _NC_CACHE[key]

    core_ids = _core_ids if _core_ids is not None else list(range(NCORES))
    in_maps = []
    for c in core_ids:
        sl = x[c * T:(c + 1) * T]
        m = {
            "xp": np.ascontiguousarray(sl[:, PI]),
            "xr": np.ascontiguousarray(sl),
            "wp": wperm, "msk": mask, "qwp": qwperm, "scp": scperm,
            "scpe": scpe, "scde": scde,
            "qzg": qz_gu, "qwd": qw_d, "qzd": qz_d, "scd": sc_d,
        }
        m.update(consts)
        in_maps.append(m)

    res = run_bass_kernel_spmd(nc, in_maps, core_ids=core_ids)
    outs = [r["out"] for r in res.results]
    if _debug:
        kernel._debug_results = res.results
    full = np.concatenate(outs, axis=0)
    if len(core_ids) < NCORES:
        return full
    return full.reshape(4, 2048, D)
